# version 04
# baseline (speedup 1.0000x reference)
import numpy as np
from contextlib import ExitStack

import concourse.bass as bass
import concourse.mybir as mybir
from concourse.bass_utils import run_bass_kernel_spmd

F32 = mybir.dt.float32
BF16 = mybir.dt.bfloat16
F16 = mybir.dt.float16
AF = mybir.ActivationFunctionType
ALU = mybir.AluOpType

NCORE = 8
T = 1024
HC = 1027
D = 2048
DIN = 4096
NH = 64
CH = 64
NCHUNK = 16
DIO = 8512
FFH = 5461
FFN_N = float(FFH * D)
SLICE_W = 11008


class Eng:
    def __init__(self, nc, ctx, name):
        self.sem = ctx.enter_context(nc.semaphore(name))
        self.n = 0

    def inc(self, instr, k=1):
        if k:
            instr.then_inc(self.sem, k)
            self.n += k
        return self.n

    def tok(self):
        return (self, self.n)


def _w(ectx, *toks):
    for t in toks:
        if t is not None:
            ectx.wait_ge(t[0].sem, t[1])


def build(tap=None):
    nc = bass.Bass("TRN2", target_bir_lowering=False)

    u_pad = nc.dram_tensor("u_pad", [HC, D], F32, kind="ExternalInput").ap()
    WinT = nc.dram_tensor("WinT", [D, DIO], F32, kind="ExternalInput").ap()
    WoutT = nc.dram_tensor("WoutT", [DIN, D], F32, kind="ExternalInput").ap()
    w1T = nc.dram_tensor("w1T", [D, FFH], F32, kind="ExternalInput").ap()
    w3T = nc.dram_tensor("w3T", [D, FFH], F32, kind="ExternalInput").ap()
    w2T = nc.dram_tensor("w2T", [FFH, D], F32, kind="ExternalInput").ap()
    wslice = nc.dram_tensor("wslice", [128, 3 * SLICE_W], F32, kind="ExternalInput").ap()
    conv_taps = nc.dram_tensor("conv_taps", [128, 136], F32, kind="ExternalInput").ap()
    conv_b_cm = nc.dram_tensor("conv_b_cm", [128, 34], F32, kind="ExternalInput").ap()
    D_cm = nc.dram_tensor("D_cm", [128, 32], F32, kind="ExternalInput").ap()
    normw_cm = nc.dram_tensor("normw_cm", [128, 32], F32, kind="ExternalInput").ap()
    dt_bias_c = nc.dram_tensor("dt_bias_c", [64, 1], F32, kind="ExternalInput").ap()
    A_log_c = nc.dram_tensor("A_log_c", [64, 1], F32, kind="ExternalInput").ap()
    idnf = nc.dram_tensor("idnf", [128, 128], F32, kind="ExternalInput").ap()
    idnb = nc.dram_tensor("idnb", [128, 128], BF16, kind="ExternalInput").ap()
    masknegd = nc.dram_tensor("masknegd", [64, 64], BF16, kind="ExternalInput").ap()
    wmaskT = nc.dram_tensor("wmaskT", [8, 8], F32, kind="ExternalInput").ap()
    qmask = nc.dram_tensor("qmask", [8, 1], F32, kind="ExternalInput").ap()
    out_d = nc.dram_tensor("out_d", [T, D], F32, kind="ExternalOutput").ap()

    dbg = None
    if tap is not None:
        dbg_shapes = {
            "zx": [128, 3072], "acm": [128, 4096], "conv": [128, 2048],
            "y": [128, 4096], "yall": [128, 32768], "ycorr": [128, 8192], "ynorm": [128, 2048], "x1": [128, 2048],
            "g": [128, 2048], "svals": [128, 16], "wout": [128, 2048], "wout2": [128, 2048], "corr2": [128, 2048], "p5": [128, 2048], "p5a": [128, 2048], "p5b": [128, 2048], "p5c": [128, 2048], "p5d": [128, 2048],
        }
        dbg = nc.dram_tensor("dbg", dbg_shapes[tap], F32, kind="ExternalOutput").ap()

    z_dram = nc.dram_tensor("z_dram", [128, 32 * T], F16)
    y_dram = nc.dram_tensor("y_dram", [128, NCHUNK * 2048], F16)
    ares_dram = nc.dram_tensor("ares_dram", [128, 16 * T], F32)
    x2_dram = nc.dram_tensor("x2_dram", [128, 16 * T], F32)
    acm_dram = nc.dram_tensor("acm_dram", [64, T], F32)
    aglob_dram = nc.dram_tensor("aglob_dram", [64, T], F32)
    alast_dram = nc.dram_tensor("alast_dram", [64, 16], F32)
    coef_dram = nc.dram_tensor("coef_dram", [8, 64], F32)
    E_loc = nc.dram_tensor("E_loc", [128, 4100], F32)
    E_all = nc.dram_tensor("E_all", [128 * NCORE, 4100], F32)
    sA_loc = nc.dram_tensor("sA_loc", [1, 4], F32)
    sA_all = nc.dram_tensor("sA_all", [1, 4], F32)

    with ExitStack() as CTX:
        E_ = CTX.enter_context
        sq = Eng(nc, CTX, "sq")
        gq = Eng(nc, CTX, "gq")
        pe = Eng(nc, CTX, "pe")
        dv = Eng(nc, CTX, "dv")
        ac = Eng(nc, CTX, "ac")
        po = Eng(nc, CTX, "po")
        cc = Eng(nc, CTX, "cc")
        ENGS = [sq, gq, pe, dv, ac, po]

        block = E_(nc.Block())

        def barrier():
            toks = [(e, e.n) for e in ENGS]
            for opener, eng in [(block.sync, sq), (block.gpsimd, gq), (block.tensor, pe),
                                (block.vector, dv), (block.scalar, ac)]:
                def f(ectx, eng=eng):
                    for (e, n) in toks:
                        if e is not eng and n > 0:
                            ectx.wait_ge(e.sem, n)
                opener(f)

        def finish(ectx, me):
            for e in ENGS:
                if e.n and e is not me:
                    ectx.wait_ge(e.sem, e.n)

        taps_s = E_(nc.sbuf_tensor("taps_s", [128, 136], F32))
        taps_bf = E_(nc.sbuf_tensor("taps_bf", [128, 136], BF16))
        convb_s = E_(nc.sbuf_tensor("convb_s", [128, 34], F32))
        Dcm_s = E_(nc.sbuf_tensor("Dcm_s", [128, 32], F32))
        nw_s = E_(nc.sbuf_tensor("nw_s", [128, 32], F32))
        dtb_s = E_(nc.sbuf_tensor("dtb_s", [64, 1], F32))
        alog_s = E_(nc.sbuf_tensor("alog_s", [64, 1], F32))
        idnf_s = E_(nc.sbuf_tensor("idnf_s", [128, 128], F32))
        idnb_s = E_(nc.sbuf_tensor("idnb_s", [128, 128], BF16))
        wmaskT_s = E_(nc.sbuf_tensor("wmaskT_s", [8, 8], F32))
        qmask_s = E_(nc.sbuf_tensor("qmask_s", [8, 1], F32))
        ones_col = E_(nc.sbuf_tensor("ones_col", [128, 1], F32))
        ones_col_b = E_(nc.sbuf_tensor("ones_col_b", [128, 1], BF16))
        ones_row = E_(nc.sbuf_tensor("ones_row", [1, 128], F32))
        ones65 = E_(nc.sbuf_tensor("ones65", [65, 128], F32))
        one64 = E_(nc.sbuf_tensor("one64", [64, 1], F32))
        eps1 = E_(nc.sbuf_tensor("eps1", [1, 1], F32))
        negA_s = E_(nc.sbuf_tensor("negA_s", [64, 1], F32))
        shalf_s = E_(nc.sbuf_tensor("shalf_s", [128, 4], F32))
        sneg_s = E_(nc.sbuf_tensor("sneg_s", [128, 4], F32))
        rstd1_b = E_(nc.sbuf_tensor("rstd1_b", [128, T], F32))
        am1_b = E_(nc.sbuf_tensor("am1_b", [128, T], F32))
        stat_row = E_(nc.sbuf_tensor("stat_row", [1, T], F32))
        stat_row2 = E_(nc.sbuf_tensor("stat_row2", [1, T], F32))
        ln_scr = E_(nc.sbuf_tensor("ln_scr", [1, T], F32))
        C_keep = E_(nc.sbuf_tensor("C_keep", [128, T], BF16))

        p_io0 = E_(nc.psum_tensor("p_io0", [128, 512], F32))
        p_io1 = E_(nc.psum_tensor("p_io1", [128, 512], F32))
        p_bc = E_(nc.psum_tensor("p_bc", [128, 512], F32))
        p_ms = E_(nc.psum_tensor("p_ms", [128, 512], F32))
        p_tb0 = E_(nc.psum_tensor("p_tb0", [128, 1024], BF16))
        p_tb1 = E_(nc.psum_tensor("p_tb1", [128, 1024], BF16))
        p_y0 = E_(nc.psum_tensor("p_y0", [128, 512], F32))
        p_y1 = E_(nc.psum_tensor("p_y1", [128, 512], F32))

        @block.sync
        def _(sync):
            for dst, src in [(taps_s, conv_taps), (convb_s, conv_b_cm), (Dcm_s, D_cm),
                             (nw_s, normw_cm), (dtb_s, dt_bias_c), (alog_s, A_log_c),
                             (idnf_s, idnf), (idnb_s, idnb),
                             (wmaskT_s, wmaskT), (qmask_s, qmask)]:
                sq.inc(sync.dma_start(out=dst[:, :], in_=src), 16)
        t_cl = sq.tok()

        @block.vector
        def _(vector):
            _w(vector, t_cl)
            vector.memset(ones_col[:, :], 1.0)
            vector.memset(ones_col_b[:, :], 1.0)
            vector.memset(ones_row[:, :], 1.0)
            vector.memset(ones65[:, :], 1.0)
            vector.memset(one64[:, :], 1.0)
            dv.inc(vector.memset(eps1[:, :], 1e-5))
        t_ms0 = dv.tok()

        @block.scalar
        def _(scalar):
            _w(scalar, t_ms0)
            ac.inc(scalar.activation(out=negA_s[:, :], in_=alog_s[:, :], func=AF.Exp))
            ac.inc(scalar.activation(out=taps_bf[:, :], in_=taps_s[:, :], func=AF.Copy))
        t_eA = ac.tok()

        @block.vector
        def _(vector):
            _w(vector, t_eA)
            dv.inc(vector.tensor_scalar(out=negA_s[:, :], in0=negA_s[:, :],
                                        scalar1=-1.0, scalar2=None, op0=ALU.mult))
        t_negA = dv.tok()

        with ExitStack() as RX:
            EX = RX.enter_context
            xbc_s = EX(nc.sbuf_tensor("xbc_s", [128, 34 * HC], BF16))
            dtv_s = EX(nc.sbuf_tensor("dtv_s", [64, T], F32))
            Acm_s = EX(nc.sbuf_tensor("Acm_s", [64, T], F32))
            acm_s = EX(nc.sbuf_tensor("acm_s", [64, T], F32))
            aglob_s = EX(nc.sbuf_tensor("aglob_s", [64, T], F32))
            alast_s = EX(nc.sbuf_tensor("alast_s", [64, 64], F32))
            O_s = EX(nc.sbuf_tensor("O_s", [64, 16], F32))

            with ExitStack() as R1:
                E1 = R1.enter_context
                u_tm = [E1(nc.sbuf_tensor(f"u_tm{i}", [128, 2048], F32)) for i in range(2)]
                u_cmb = E1(nc.sbuf_tensor("u_cmb", [128, 16 * HC], BF16))
                wf = [E1(nc.sbuf_tensor(f"wf{i}", [128, 16 * 128], F32)) for i in range(2)]
                wb = [E1(nc.sbuf_tensor(f"wb{i}", [128, 16 * 128], BF16)) for i in range(2)]
                z_st = [E1(nc.sbuf_tensor(f"z_st{i}", [128, 512], F16)) for i in range(2)]
                dtraw_s = E1(nc.sbuf_tensor("dtraw_s", [64, T], F32))
                scr64 = E1(nc.sbuf_tensor("scr64", [64, T], F32))

                udma_t = [None, None]
                uev_t = [None, None]
                bank_free = [None, None]
                for i in range(9):
                    rows = 128 if i < 8 else 3
                    par = i % 2

                    @block.sync
                    def _(sync, i=i, rows=rows, par=par):
                        _w(sync, uev_t[par])
                        sq.inc(sync.dma_start(out=u_tm[par][0:rows, :],
                                              in_=u_pad[i * 128:i * 128 + rows, :]), 16)
                    udma_t[par] = sq.tok()

                    for grp in range(4):
                        bank = [p_io0, p_io1][grp % 2]

                        @block.tensor
                        def _(tensor, rows=rows, par=par, grp=grp, bank=bank):
                            _w(tensor, udma_t[par], bank_free[grp % 2])
                            for j in range(4):
                                g = grp * 4 + j
                                pe.inc(tensor.transpose(bank[0:128, j * 128:j * 128 + rows],
                                                        u_tm[par][0:rows, g * 128:(g + 1) * 128],
                                                        idnf_s[0:rows, 0:rows]))
                        t_tp = pe.tok()

                        @block.vector
                        def _(vector, i=i, rows=rows, grp=grp, bank=bank, t_tp=t_tp):
                            _w(vector, t_tp)
                            for j in range(4):
                                g = grp * 4 + j
                                dv.inc(vector.tensor_copy(
                                    out=u_cmb[:, g * HC + i * 128: g * HC + i * 128 + rows],
                                    in_=bank[0:128, j * 128:j * 128 + rows]))
                        bank_free[grp % 2] = dv.tok()
                    uev_t[par] = dv.tok()
                u_ready = dv.tok()

                zxdt_tiles = [(3, 512), (515, 512)]
                xbc_tiles = [(0, 343), (343, 342), (685, 342)]
                blocks = [(m, 1) for m in range(67)]

                wdma_t = [None, None]
                wcast_t = [None, None]
                wused_t = [None, None]
                io_free = [bank_free[0], bank_free[1]]
                zst_free = [None, None]
                zct = 0
                for bi, (m0, mcnt) in enumerate(blocks):
                    par = bi % 2
                    mw = 128 * mcnt if m0 != 66 else 64

                    @block.sync
                    def _(sync, m0=m0, mw=mw, par=par):
                        _w(sync, wcast_t[par])
                        src = WinT.rearrange("(kc kp) m -> kp kc m", kp=128)[:, :, m0 * 128:m0 * 128 + mw]
                        dst = wf[par][:, 0:16 * mw].rearrange("p (kc m) -> p kc m", kc=16)
                        sq.inc(sync.dma_start(out=dst, in_=src), 16)
                    wdma_t[par] = sq.tok()

                    @block.gpsimd
                    def _(gpsimd, mw=mw, par=par):
                        _w(gpsimd, wdma_t[par], wused_t[par])
                        po.inc(gpsimd.tensor_copy(out=wb[par][:, 0:16 * mw], in_=wf[par][:, 0:16 * mw]))
                    wcast_t[par] = po.tok()

                    for mi in range(mcnt):
                        m = m0 + mi
                        rows = 64 if m == 66 else 128
                        tiles = xbc_tiles if 32 <= m < 66 else zxdt_tiles
                        for ti, (t0, tn) in enumerate(tiles):
                            bpar = zct % 2
                            zct += 1
                            bank = [p_io0, p_io1][bpar]

                            @block.tensor
                            def _(tensor, bi=bi, m=m, mi=mi, mw=mw, par=par, rows=rows,
                                  t0=t0, tn=tn, bank=bank, bpar=bpar):
                                _w(tensor, wcast_t[par], io_free[bpar])
                                if bi == 0:
                                    _w(tensor, u_ready)
                                moff = mi * 128
                                for k in range(16):
                                    lhsT = wb[par][:, k * mw + moff: k * mw + moff + rows]
                                    rhs = u_cmb[:, k * HC + t0: k * HC + t0 + tn]
                                    pe.inc(tensor.matmul(bank[0:rows, 0:tn], lhsT, rhs,
                                                         start=(k == 0), stop=(k == 15)),
                                           1 if k == 15 else 0)
                            t_mm = pe.tok()
                            wused_t[par] = t_mm

                            if m < 32:
                                zp = zct % 2

                                @block.scalar
                                def _(scalar, bank=bank, t_mm=t_mm, tn=tn, zp=zp):
                                    _w(scalar, t_mm, zst_free[zp])
                                    ac.inc(scalar.activation(out=z_st[zp][:, 0:tn], in_=bank[:, 0:tn],
                                                             func=AF.Silu))
                                t_z = ac.tok()
                                io_free[bpar] = t_z

                                @block.gpsimd
                                def _(gpsimd, m=m, t0=t0, tn=tn, zp=zp, t_z=t_z):
                                    _w(gpsimd, t_z)
                                    gq.inc(gpsimd.dma_start(
                                        out=z_dram.ap()[:, m * T + (t0 - 3): m * T + (t0 - 3) + tn],
                                        in_=z_st[zp][:, 0:tn]), 16)
                                zst_free[zp] = gq.tok()
                            elif m < 66:
                                @block.scalar
                                def _(scalar, bank=bank, t_mm=t_mm, m=m, t0=t0, tn=tn):
                                    _w(scalar, t_mm)
                                    g = m - 32
                                    ac.inc(scalar.activation(out=xbc_s[:, g * HC + t0: g * HC + t0 + tn],
                                                             in_=bank[:, 0:tn], func=AF.Copy))
                                io_free[bpar] = ac.tok()
                            else:
                                @block.scalar
                                def _(scalar, bank=bank, t_mm=t_mm, t0=t0, tn=tn):
                                    _w(scalar, t_mm)
                                    ac.inc(scalar.activation(out=dtraw_s[:, t0 - 3: t0 - 3 + tn],
                                                             in_=bank[0:64, 0:tn], func=AF.Copy))
                                io_free[bpar] = ac.tok()
                z_gq_done = gq.tok()
                p2_ac = ac.tok()

                if tap == "zx":
                    @block.gpsimd
                    def _(gpsimd):
                        finish(gpsimd, gq)
                        gq.inc(gpsimd.dma_start(out=dbg[:, 0:1024], in_=z_dram.ap()[:, 0:1024]), 16)
                        gq.inc(gpsimd.dma_start(out=dbg[:, 1024:2048], in_=xbc_s[:, 0:1024]), 16)
                        gq.inc(gpsimd.dma_start(out=dbg[0:64, 2048:3072], in_=dtraw_s[:, :]), 16)
                        gpsimd.wait_ge(gq.sem, gq.n)
                    return nc

                @block.vector
                def _(vector):
                    _w(vector, p2_ac, t_negA)
                    dv.inc(vector.tensor_scalar(out=dtraw_s[:, :], in0=dtraw_s[:, :],
                                                scalar1=dtb_s[:, :], scalar2=None, op0=ALU.add))
                    dv.inc(vector.scalar_tensor_tensor(out=scr64[:, :], in0=dtraw_s[:, :],
                                                       scalar=-1.0, in1=dtraw_s[:, :],
                                                       op0=ALU.mult, op1=ALU.max))
                t_sp1 = dv.tok()

                @block.scalar
                def _(scalar):
                    _w(scalar, t_sp1)
                    ac.inc(scalar.activation(out=scr64[:, :], in_=scr64[:, :], func=AF.Exp, scale=-1.0))
                    ac.inc(scalar.activation(out=scr64[:, :], in_=scr64[:, :], func=AF.Ln, bias=one64[:, :]))
                t_sp2 = ac.tok()

                @block.vector
                def _(vector):
                    _w(vector, t_sp2)
                    dv.inc(vector.tensor_scalar(out=dtraw_s[:, :], in0=dtraw_s[:, :],
                                                scalar1=0.0, scalar2=None, op0=ALU.max))
                    dv.inc(vector.tensor_tensor(out=dtv_s[:, :], in0=dtraw_s[:, :], in1=scr64[:, :],
                                                op=ALU.add))
                    dv.inc(vector.tensor_scalar(out=Acm_s[:, :], in0=dtv_s[:, :],
                                                scalar1=negA_s[:, :], scalar2=None, op0=ALU.mult))
                    vector.wait_ge(dv.sem, dv.n)
                    for c in range(NCHUNK):
                        dv.inc(vector.tensor_tensor_scan(out=acm_s[:, c * CH:(c + 1) * CH],
                                                         data0=one64[:, :].broadcast_to([64, CH]),
                                                         data1=Acm_s[:, c * CH:(c + 1) * CH],
                                                         initial=0.0, op0=ALU.mult, op1=ALU.add))
                    vector.memset(alast_s[:, :], 0.0)
                    dv.inc(vector.tensor_copy(out=alast_s[:, 0:16], in_=acm_s[:, CH - 1::CH]))
                    vector.wait_ge(dv.sem, dv.n)
                    dv.inc(vector.tensor_tensor_scan(out=scr64[:, 0:64],
                                                     data0=one64[:, :].broadcast_to([64, 64]),
                                                     data1=alast_s[:, 0:64], initial=0.0,
                                                     op0=ALU.mult, op1=ALU.add))
                    vector.wait_ge(dv.sem, dv.n)
                    vector.memset(O_s[:, 0:1], 0.0)
                    dv.inc(vector.tensor_copy(out=O_s[:, 1:16], in_=scr64[:, 0:15]))
                    vector.wait_ge(dv.sem, dv.n)
                    dv.inc(vector.tensor_tensor(
                        out=aglob_s.ap().rearrange("p (c l) -> p c l", c=16),
                        in0=acm_s.ap().rearrange("p (c l) -> p c l", c=16),
                        in1=O_s[:, :, None].broadcast_to([64, 16, CH]), op=ALU.add))
                t_acm = dv.tok()

                @block.sync
                def _(sync):
                    _w(sync, t_acm)
                    sq.inc(sync.dma_start(out=acm_dram.ap(), in_=acm_s[:, :]), 16)
                    sq.inc(sync.dma_start(out=aglob_dram.ap(), in_=aglob_s[:, :]), 16)
                    sq.inc(sync.dma_start(out=alast_dram.ap(), in_=alast_s[:, 0:16]), 16)
                    sq.inc(sync.dma_start(out=E_loc.ap()[0:64, 4096:4097], in_=scr64[:, 15:16], allow_slow_non_contiguous=True), 16)
                t_acmdma = sq.tok()

                if tap == "acm":
                    @block.gpsimd
                    def _(gpsimd):
                        finish(gpsimd, gq)
                        gq.inc(gpsimd.dma_start(out=dbg[0:64, 0:1024], in_=dtv_s[:, :]), 16)
                        gq.inc(gpsimd.dma_start(out=dbg[0:64, 1024:2048], in_=Acm_s[:, :]), 16)
                        gq.inc(gpsimd.dma_start(out=dbg[0:64, 2048:3072], in_=acm_s[:, :]), 16)
                        gq.inc(gpsimd.dma_start(out=dbg[0:64, 3072:4096], in_=aglob_s[:, :]), 16)
                        gq.inc(gpsimd.dma_start(out=dbg[64:128, 0:64], in_=alast_s[:, :]), 16)
                        gq.inc(gpsimd.dma_start(out=dbg[64:128, 64:128], in_=scr64[:, 0:64]), 16)
                        gq.inc(gpsimd.dma_start(out=dbg[64:128, 128:144], in_=O_s[:, :]), 16)
                        gpsimd.wait_ge(gq.sem, gq.n)
                    return nc

                barrier()

            with ExitStack() as R2:
                E2 = R2.enter_context
                ct = [E2(nc.sbuf_tensor(f"ct{i}", [128, T], F32)) for i in range(2)]
                wsl = E2(nc.sbuf_tensor("wsl", [128, SLICE_W], F32))
                red86 = E2(nc.sbuf_tensor("red86", [128, 86], F32))
                red1 = E2(nc.sbuf_tensor("red1", [128, 4], F32))
                sA_s = E2(nc.sbuf_tensor("sA_s", [1, 4], F32))
                svals = E2(nc.sbuf_tensor("svals", [1, 4], F32))

                conv_dve_t = [None] * 34
                conv_act_t = [None] * 34
                for g in range(34):
                    par = g % 2

                    @block.vector
                    def _(vector, g=g, par=par):
                        if g >= 2:
                            _w(vector, conv_act_t[g - 2])
                        gb = g * HC
                        vector.tensor_tensor(out=ct[par][:, :], in0=xbc_s[:, gb: gb + T],
                                             in1=taps_bf[:, g:g + 1].broadcast_to([128, T]), op=ALU.mult)
                        ins = None
                        for k in (1, 2, 3):
                            ins = vector.scalar_tensor_tensor(out=ct[par][:, :],
                                                              in0=xbc_s[:, gb + k: gb + k + T],
                                                              scalar=taps_bf[:, k * 34 + g: k * 34 + g + 1],
                                                              in1=ct[par][:, :], op0=ALU.mult, op1=ALU.add)
                        dv.inc(ins)
                    conv_dve_t[g] = dv.tok()

                    @block.scalar
                    def _(scalar, g=g, par=par):
                        _w(scalar, conv_dve_t[g])
                        ac.inc(scalar.activation(out=xbc_s[:, g * HC: g * HC + T], in_=ct[par][:, :],
                                                 func=AF.Silu, bias=convb_s[:, g:g + 1]))
                    conv_act_t[g] = ac.tok()
                conv_done = ac.tok()

                wsl_free = None
                for wi in range(3):
                    @block.sync
                    def _(sync, wi=wi):
                        _w(sync, wsl_free)
                        sq.inc(sync.dma_start(out=wsl[:, :],
                                              in_=wslice[:, wi * SLICE_W:(wi + 1) * SLICE_W]), 16)
                    t_wd = sq.tok()

                    @block.vector
                    def _(vector, wi=wi, t_wd=t_wd):
                        _w(vector, t_wd)
                        dv.inc(vector.tensor_reduce(out=red86[:, :],
                                                    in_=wsl.ap().rearrange("p (a b) -> p a b", a=86),
                                                    axis=mybir.AxisListType.X, op=ALU.add,
                                                    apply_absolute_value=True))
                        dv.inc(vector.tensor_reduce(out=red1[:, wi:wi + 1], in_=red86[:, :],
                                                    axis=mybir.AxisListType.X, op=ALU.add))
                    wsl_free = dv.tok()
                t_redA = dv.tok()

                @block.tensor
                def _(tensor):
                    _w(tensor, t_redA)
                    for wi in range(3):
                        pe.inc(tensor.matmul(p_ms[0:1, wi:wi + 1], ones_col[:, :], red1[:, wi:wi + 1],
                                             start=True, stop=True))
                t_sApe = pe.tok()

                @block.scalar
                def _(scalar):
                    _w(scalar, t_sApe)
                    ac.inc(scalar.activation(out=sA_s[0:1, 0:4], in_=p_ms[0:1, 0:4], func=AF.Copy))
                t_sA = ac.tok()

                @block.gpsimd
                def _(gpsimd):
                    _w(gpsimd, t_sA)
                    gq.inc(gpsimd.dma_start(out=sA_loc.ap(), in_=sA_s[0:1, :]), 16)
                    gpsimd.wait_ge(gq.sem, gq.n)
                    cc.inc(gpsimd.collective_compute(
                        "AllReduce", ALU.add, replica_groups=[list(range(NCORE))],
                        ins=[sA_loc.ap().opt()], outs=[sA_all.ap().opt()]))
                    gpsimd.wait_ge(cc.sem, cc.n)
                    gq.inc(gpsimd.dma_start(out=sA_s[0:1, :], in_=sA_all.ap()), 16)
                t_sAr = gq.tok()

                @block.vector
                def _(vector):
                    _w(vector, t_sAr)
                    dv.inc(vector.tensor_scalar(out=svals[:, :], in0=sA_s[0:1, :],
                                                scalar1=1.0 / FFN_N, scalar2=None, op0=ALU.mult))
                t_sv = dv.tok()

                @block.tensor
                def _(tensor):
                    _w(tensor, t_sv, t_sApe)
                    pe.inc(tensor.matmul(p_ms[0:128, 4:8], ones_row[:, :], svals[0:1, :],
                                         start=True, stop=True))
                t_svb = pe.tok()

                @block.vector
                def _(vector):
                    _w(vector, t_svb)
                    dv.inc(vector.tensor_scalar(out=shalf_s[:, :], in0=p_ms[:, 4:8],
                                                scalar1=0.5, scalar2=None, op0=ALU.mult))
                    dv.inc(vector.tensor_scalar(out=sneg_s[:, :], in0=p_ms[:, 4:8],
                                                scalar1=-0.5, scalar2=None, op0=ALU.mult))
                t_shalf = dv.tok()

                if tap == "conv":
                    @block.gpsimd
                    def _(gpsimd):
                        finish(gpsimd, gq)
                        gq.inc(gpsimd.dma_start(out=dbg[:, 0:1024], in_=xbc_s[:, 0:1024]), 16)
                        gq.inc(gpsimd.dma_start(out=dbg[:, 1024:2048], in_=xbc_s[:, 33 * HC:33 * HC + 1024]), 16)
                        gpsimd.wait_ge(gq.sem, gq.n)
                    return nc
                if tap == "svals":
                    @block.gpsimd
                    def _(gpsimd):
                        finish(gpsimd, gq)
                        gq.inc(gpsimd.dma_start(out=dbg[:, 0:4], in_=shalf_s[:, :]), 16)
                        gq.inc(gpsimd.dma_start(out=dbg[0:1, 4:8], in_=sA_s[0:1, :]), 16)
                        gpsimd.wait_ge(gq.sem, gq.n)
                    return nc
                barrier()

            with ExitStack() as R3:
                E3 = R3.enter_context
                mneg_s = E3(nc.sbuf_tensor("mneg_s", [64, 64], BF16))
                dbc_s = E3(nc.sbuf_tensor("dbc_s", [128, 1024], F32))
                aflat = E3(nc.sbuf_tensor("aflat", [1, 4096], F32))
                Ltmp = E3(nc.sbuf_tensor("Ltmp", [64, 4096], F32))
                M_bf = E3(nc.sbuf_tensor("M_bf", [64, 4096], BF16))
                Csc = E3(nc.sbuf_tensor("Csc", [128, 4096], BF16))
                x2_s = E3(nc.sbuf_tensor("x2_s", [64, 4096], BF16))
                x2d_s = E3(nc.sbuf_tensor("x2d_s", [64, 4096], BF16))
                at_s = E3(nc.sbuf_tensor("at_s", [64, 64], F32))
                dtt_s = E3(nc.sbuf_tensor("dtt_s", [64, 64], F32))
                Wcm_s = E3(nc.sbuf_tensor("Wcm_s", [64, 64], F32))
                Wt_s = E3(nc.sbuf_tensor("Wt_s", [64, 64], F32))
                Bt_s = E3(nc.sbuf_tensor("Bt_s", [64, 128], BF16))
                S_run = E3(nc.sbuf_tensor("S_run", [128, 4096], F32))
                S_in_bf = E3(nc.sbuf_tensor("S_in_bf", [128, 4096], BF16))
                y_st = [E3(nc.sbuf_tensor("y_st0", [128, 2048], F16))] * 2

                @block.sync
                def _(sync):
                    sq.inc(sync.dma_start(out=mneg_s[:, :], in_=masknegd), 16)
                t_mneg = sq.tok()

                @block.gpsimd
                def _(gpsimd):
                    _w(gpsimd, t_acmdma)
                    gq.inc(gpsimd.dma_start(out=aflat[0:1, 0:1024],
                                            in_=alast_dram.ap().rearrange("h c -> (h c)").unsqueeze(0)), 16)
                t_af = gq.tok()

                @block.tensor
                def _(tensor):
                    _w(tensor, t_af, io_free[0], io_free[1])
                    pe.inc(tensor.matmul(p_io0[:, 0:512], ones_row[:, :], aflat[0:1, 0:512],
                                         start=True, stop=True))
                    pe.inc(tensor.matmul(p_io1[:, 0:512], ones_row[:, :], aflat[0:1, 512:1024],
                                         start=True, stop=True))
                t_dbc = pe.tok()

                @block.scalar
                def _(scalar):
                    _w(scalar, t_dbc)
                    ac.inc(scalar.activation(out=dbc_s[:, 0:512], in_=p_io0[:, 0:512], func=AF.Exp))
                    ac.inc(scalar.activation(out=dbc_s[:, 512:1024], in_=p_io1[:, 0:512], func=AF.Exp))
                t_dbcs = ac.tok()

                @block.vector
                def _(vector):
                    _w(vector, t_dbcs)
                    dv.inc(vector.memset(S_run[:, :], 0.0))
                scan_t = dv.tok()

                aflat_t = [None, None]
                apex_t = [None, None]
                ydma_t = [None, None]
                cast_t = None
                yB_free = [None, None]
                sB_free = [t_dbcs, t_dbcs]
                last_ymm = None
                last_Mbf = None
                last_csc = None
                last_x2d = None
                last_smm = None
                last_w4 = None
                xev_tok = [None, None]

                for c in range(NCHUNK):
                    par = c % 2
                    arow = 0

                    @block.gpsimd
                    def _(gpsimd, c=c):
                        _w(gpsimd, t_acmdma, apex_t[0])
                        src = acm_dram.ap()[None, :, c * CH:(c + 1) * CH]
                        dst = aflat[0:1, :].rearrange("p (h l) -> p h l", h=64)
                        gq.inc(gpsimd.dma_start(out=dst, in_=src), 16)
                    aflat_t[0] = gq.tok()
                    aflat_t[1] = aflat_t[0]

                    for j in range(8):
                        ybk = j % 2
                        bank = [p_y0, p_y1][ybk]

                        @block.tensor
                        def _(tensor, par=par, j=j, bank=bank, ybk=ybk, arow=arow):
                            _w(tensor, aflat_t[par], yB_free[ybk])
                            pe.inc(tensor.matmul(bank[0:64, 0:512], ones_row[0:1, 0:64],
                                                 aflat[0:1, j * 512:(j + 1) * 512],
                                                 start=True, stop=True))
                        t_R = pe.tok()

                        @block.vector
                        def _(vector, j=j, bank=bank, t_R=t_R):
                            _w(vector, t_R, t_mneg)
                            dv.inc(vector.tensor_tensor(
                                out=Ltmp[:, j * 512:(j + 1) * 512].rearrange("p (h l) -> p h l", h=8),
                                in0=bank[0:64, 0:512].rearrange("p (h l) -> p h l", h=8),
                                in1=mneg_s[:, None, :].broadcast_to([64, 8, 64]), op=ALU.add))
                        yB_free[ybk] = dv.tok()

                    @block.tensor
                    def _(tensor, c=c):
                        _w(tensor, t_acm, last_Mbf)
                        pe.inc(tensor.transpose(p_bc[0:64, 0:64], acm_s[:, c * CH:(c + 1) * CH],
                                                idnf_s[0:64, 0:64]))
                        pe.inc(tensor.transpose(p_bc[0:64, 64:128], dtv_s[:, c * CH:(c + 1) * CH],
                                                idnf_s[0:64, 0:64]))
                    t_tpa = pe.tok()

                    @block.vector
                    def _(vector, t_tpa=t_tpa):
                        _w(vector, t_tpa)
                        dv.inc(vector.tensor_copy(out=at_s[:, :], in_=p_bc[0:64, 0:64]))
                        dv.inc(vector.tensor_copy(out=dtt_s[:, :], in_=p_bc[0:64, 64:128]))
                    t_atd = dv.tok()

                    @block.vector
                    def _(vector, c=c, lw4=last_w4):
                        _w(vector, lw4)
                        dv.inc(vector.tensor_scalar(out=Wcm_s[:, :], in0=acm_s[:, c * CH:(c + 1) * CH],
                                                    scalar1=alast_s[:, c:c + 1], scalar2=None,
                                                    op0=ALU.subtract))
                    t_w1 = dv.tok()

                    @block.scalar
                    def _(scalar, t_w1=t_w1):
                        _w(scalar, t_w1)
                        ac.inc(scalar.activation(out=Wcm_s[:, :], in_=Wcm_s[:, :], func=AF.Exp, scale=-1.0))
                    t_w2 = ac.tok()

                    @block.vector
                    def _(vector, c=c, t_w2=t_w2):
                        _w(vector, t_w2)
                        dv.inc(vector.tensor_tensor(out=Wcm_s[:, :], in0=Wcm_s[:, :],
                                                    in1=dtv_s[:, c * CH:(c + 1) * CH], op=ALU.mult))
                    t_w3 = dv.tok()

                    @block.tensor
                    def _(tensor, t_w3=t_w3):
                        _w(tensor, t_w3)
                        pe.inc(tensor.transpose(p_ms[0:64, 0:64], Wcm_s[:, :], idnf_s[0:64, 0:64]))
                    t_w4 = pe.tok()
                    last_w4 = t_w4

                    @block.vector
                    def _(vector, t_w4=t_w4):
                        _w(vector, t_w4)
                        dv.inc(vector.tensor_copy(out=Wt_s[:, :], in_=p_ms[0:64, 0:64]))
                    t_wt = dv.tok()

                    @block.tensor
                    def _(tensor, c=c, t_wt=t_wt):
                        _w(tensor, conv_done, t_wt)
                        pe.inc(tensor.matmul(p_ms[0:64, 64:128],
                                             xbc_s[:, 32 * HC + c * CH: 32 * HC + c * CH + CH],
                                             xbc_s[:, 33 * HC + c * CH: 33 * HC + c * CH + CH],
                                             start=True, stop=True))
                    t_G = pe.tok()

                    @block.vector
                    def _(vector, t_atd=t_atd):
                        _w(vector, t_atd)
                        dv.inc(vector.tensor_tensor(
                            out=Ltmp.ap().rearrange("p (h l) -> p h l", h=64),
                            in0=Ltmp.ap().rearrange("p (h l) -> p h l", h=64),
                            in1=at_s[:, :, None].broadcast_to([64, 64, 64]), op=ALU.subtract))
                    t_L1 = dv.tok()

                    @block.scalar
                    def _(scalar, t_L1=t_L1):
                        _w(scalar, t_L1)
                        ac.inc(scalar.activation(out=Ltmp[:, :], in_=Ltmp[:, :], func=AF.Exp))
                    t_L2 = ac.tok()

                    @block.vector
                    def _(vector, t_L2=t_L2, t_G=t_G, lym=last_ymm):
                        _w(vector, t_L2, t_G, lym)
                        dv.inc(vector.tensor_tensor(
                            out=Ltmp.ap().rearrange("p (h l) -> p h l", h=64),
                            in0=Ltmp.ap().rearrange("p (h l) -> p h l", h=64),
                            in1=dtt_s[:, :, None].broadcast_to([64, 64, 64]), op=ALU.mult))
                        dv.inc(vector.tensor_tensor(
                            out=M_bf.ap().rearrange("p (h l) -> p h l", h=64),
                            in0=Ltmp.ap().rearrange("p (h l) -> p h l", h=64),
                            in1=p_ms[0:64, None, 64:128].broadcast_to([64, 64, 64]), op=ALU.mult))
                    last_Mbf = dv.tok()

                    for grp in range(4):
                        tbk = grp % 2
                        tb = [p_tb0, p_tb1][tbk]

                        @block.tensor
                        def _(tensor, c=c, grp=grp, tb=tb, tbk=tbk, lx2d=last_x2d):
                            _w(tensor, conv_done, xev_tok[tbk])
                            if grp == 0:
                                _w(tensor, lx2d)
                            for j in range(8):
                                g = grp * 8 + j
                                pe.inc(tensor.transpose(tb[0:64, j * 128:(j + 1) * 128],
                                                        xbc_s[:, g * HC + c * CH: g * HC + c * CH + CH],
                                                        idnb_s[:, :]))
                        t_xtp = pe.tok()

                        @block.scalar
                        def _(scalar, grp=grp, tb=tb, t_xtp=t_xtp):
                            _w(scalar, t_xtp)
                            for j in range(8):
                                g = grp * 8 + j
                                ac.inc(scalar.activation(out=x2_s[:, g * 128:(g + 1) * 128],
                                                         in_=tb[0:64, j * 128:(j + 1) * 128], func=AF.Copy))
                        xev_tok[tbk] = ac.tok()
                    t_x2 = ac.tok()

                    @block.vector
                    def _(vector, t_x2=t_x2, t_wt=t_wt, lsmm=last_smm):
                        _w(vector, t_x2, t_wt, lsmm)
                        dv.inc(vector.tensor_tensor(
                            out=x2d_s.ap().rearrange("p (h q) -> p h q", h=64),
                            in0=x2_s.ap().rearrange("p (h q) -> p h q", h=64),
                            in1=Wt_s[:, :, None].broadcast_to([64, 64, 64]), op=ALU.mult))
                    last_x2d = dv.tok()

                    @block.tensor
                    def _(tensor, c=c):
                        _w(tensor, conv_done, xev_tok[0])
                        pe.inc(tensor.transpose(p_tb0[0:64, 0:128],
                                                xbc_s[:, 32 * HC + c * CH: 32 * HC + c * CH + CH],
                                                idnb_s[:, :]))
                    t_Btp = pe.tok()

                    @block.scalar
                    def _(scalar, t_Btp=t_Btp):
                        _w(scalar, t_Btp)
                        ac.inc(scalar.activation(out=Bt_s[:, :], in_=p_tb0[0:64, 0:128], func=AF.Copy))
                    t_Bt = ac.tok()
                    xev_tok[0] = t_Bt

                    for j in range(8):
                        ybk = j % 2
                        bank = [p_y0, p_y1][ybk]

                        @block.tensor
                        def _(tensor, par=par, j=j, bank=bank, ybk=ybk, arow=arow):
                            _w(tensor, aflat_t[par], yB_free[ybk])
                            pe.inc(tensor.matmul(bank[0:128, 0:512], ones_row[:, :],
                                                 aflat[0:1, j * 512:(j + 1) * 512],
                                                 start=True, stop=True))
                        t_e = pe.tok()

                        @block.scalar
                        def _(scalar, j=j, bank=bank, t_e=t_e, lcsc=last_csc):
                            _w(scalar, t_e, lcsc)
                            ac.inc(scalar.activation(out=Csc[:, j * 512:(j + 1) * 512],
                                                     in_=bank[0:128, 0:512], func=AF.Exp))
                        yB_free[ybk] = ac.tok()
                    apex_t[0] = pe.tok()
                    t_expa = ac.tok()

                    @block.vector
                    def _(vector, c=c, t_expa=t_expa, lym=last_ymm):
                        _w(vector, t_expa, lym)
                        dv.inc(vector.tensor_tensor(
                            out=Csc.ap().rearrange("p (h l) -> p h l", h=64),
                            in0=Csc.ap().rearrange("p (h l) -> p h l", h=64),
                            in1=xbc_s[:, None, 33 * HC + c * CH: 33 * HC + c * CH + CH].broadcast_to([128, 64, 64]),
                            op=ALU.mult))
                    last_csc = dv.tok()

                    @block.gpsimd
                    def _(gpsimd, scan_t=scan_t, lym=last_ymm):
                        _w(gpsimd, scan_t, lym)
                        po.inc(gpsimd.tensor_copy(out=S_in_bf[:, :], in_=S_run[:, :]))
                    cast_t = po.tok()

                    for half in range(2):
                        for sub in range(2):
                            bank = [p_y0, p_y1][sub]

                            @block.tensor
                            def _(tensor, half=half, sub=sub, bank=bank,
                                  lM=last_Mbf, lC=last_csc, ct_=cast_t, tx2=t_x2):
                                _w(tensor, lM, lC, ct_, tx2, yB_free[sub])
                                for j in range(8):
                                    hp = half * 16 + sub * 8 + j
                                    for hh in range(2):
                                        h = 2 * hp + hh
                                        o = bank[hh * 64:hh * 64 + 64, j * 64:(j + 1) * 64]
                                        tensor.matmul(o, x2_s[:, hp * 128 + hh * 64: hp * 128 + hh * 64 + 64],
                                                      M_bf[:, h * 64:(h + 1) * 64], start=True, stop=False)
                                        pe.inc(tensor.matmul(o, S_in_bf[:, h * 64:(h + 1) * 64],
                                                             Csc[:, h * 64:(h + 1) * 64],
                                                             start=False, stop=True))
                            t_ymm = pe.tok()

                            @block.vector
                            def _(vector, c=c, half=half, sub=sub, bank=bank, t_ymm=t_ymm, par=par):
                                _w(vector, t_ymm)
                                if half == 0 and sub == 0:
                                    _w(vector, ydma_t[0])
                                for j in range(8):
                                    hp = half * 16 + sub * 8 + j
                                    dv.inc(vector.scalar_tensor_tensor(
                                        out=y_st[par][:, hp * 64:(hp + 1) * 64],
                                        in0=xbc_s[:, hp * HC + c * CH: hp * HC + c * CH + CH],
                                        scalar=Dcm_s[:, hp:hp + 1],
                                        in1=bank[:, j * 64:(j + 1) * 64],
                                        op0=ALU.mult, op1=ALU.add))
                            yB_free[sub] = dv.tok()
                    last_ymm = pe.tok()
                    t_yev = dv.tok()

                    @block.gpsimd
                    def _(gpsimd, c=c, par=par, t_yev=t_yev):
                        _w(gpsimd, t_yev)
                        gq.inc(gpsimd.dma_start(out=y_dram.ap()[:, c * 2048:(c + 1) * 2048],
                                                in_=y_st[par][:, :]), 16)
                    ydma_t[0] = gq.tok()
                    ydma_t[1] = gq.tok()

                    @block.vector
                    def _(vector, c=c, ct_=cast_t):
                        _w(vector, ct_)
                        dv.inc(vector.tensor_tensor(
                            out=S_run.ap().rearrange("p (h q) -> p h q", h=64),
                            in0=S_run.ap().rearrange("p (h q) -> p h q", h=64),
                            in1=dbc_s[:, c::16][:, :, None].broadcast_to([128, 64, 64]),
                            op=ALU.mult))
                    t_decS = dv.tok()

                    for j in range(8):
                        sbk = j % 2
                        bank = [p_io0, p_io1][sbk]

                        @block.tensor
                        def _(tensor, j=j, bank=bank, sbk=sbk, t_Bt=t_Bt, lx2d=last_x2d):
                            _w(tensor, t_Bt, lx2d, sB_free[sbk])
                            pe.inc(tensor.matmul(bank[0:128, 0:512], Bt_s[:, :],
                                                 x2d_s[:, j * 512:(j + 1) * 512], start=True, stop=True))
                        t_smm = pe.tok()

                        @block.vector
                        def _(vector, j=j, bank=bank, t_smm=t_smm, t_decS=t_decS):
                            _w(vector, t_smm, t_decS)
                            dv.inc(vector.tensor_tensor(out=S_run[:, j * 512:(j + 1) * 512],
                                                        in0=S_run[:, j * 512:(j + 1) * 512],
                                                        in1=bank[0:128, 0:512], op=ALU.add))
                        sB_free[sbk] = dv.tok()
                    last_smm = pe.tok()
                    scan_t = dv.tok()

                @block.gpsimd
                def _(gpsimd, scan_t=scan_t):
                    _w(gpsimd, scan_t)
                    gq.inc(gpsimd.dma_start(out=E_loc.ap()[:, 0:4096], in_=S_run[:, :]), 16)
                    gpsimd.wait_ge(gq.sem, gq.n)
                    cc.inc(gpsimd.collective_compute(
                        "AllGather", ALU.bypass, replica_groups=[list(range(NCORE))],
                        ins=[E_loc.ap().opt()],
                        outs=[E_all.ap().opt()]))
                t_gather = cc.tok()

                @block.gpsimd
                def _(gpsimd):
                    _w(gpsimd, conv_done)
                    po.inc(gpsimd.tensor_copy(out=C_keep[:, :], in_=xbc_s[:, 33 * HC: 33 * HC + T]))
                t_ck = po.tok()

                if tap == "yall":
                    @block.gpsimd
                    def _(gpsimd):
                        finish(gpsimd, gq)
                        gpsimd.wait_ge(cc.sem, cc.n)
                        for jj in range(8):
                            gq.inc(gpsimd.dma_start(out=dbg[:, jj * 4096:(jj + 1) * 4096],
                                                    in_=y_dram.ap()[:, jj * 4096:(jj + 1) * 4096]), 16)
                        gpsimd.wait_ge(gq.sem, gq.n)
                    return nc
                if tap == "y":
                    @block.gpsimd
                    def _(gpsimd):
                        finish(gpsimd, gq)
                        gpsimd.wait_ge(cc.sem, cc.n)
                        gq.inc(gpsimd.dma_start(out=dbg[:, 0:4096], in_=y_dram.ap()[:, 0:4096]), 16)
                        gpsimd.wait_ge(gq.sem, gq.n)
                    return nc
                barrier()
        barrier()

        with ExitStack() as R4:
            E4 = R4.enter_context
            Eq_s = E4(nc.sbuf_tensor("Eq_s", [128, 4096], F32))
            S_core = E4(nc.sbuf_tensor("S_core", [128, 4096], F32))
            ynorm_h = E4(nc.sbuf_tensor("ynorm_h", [128, 32 * 512], BF16))
            ytile = E4(nc.sbuf_tensor("ytile", [128, 16384], F16))
            ztile = E4(nc.sbuf_tensor("ztile", [128, 16384], F16))
            agrow = E4(nc.sbuf_tensor("agrow", [65, 512], F32))
            ysq_bf = E4(nc.sbuf_tensor("ysq_bf", [128, 512], BF16))
            Cg_bf = ysq_bf
            rstd_b = E4(nc.sbuf_tensor("rstd_b", [128, 64], F32))
            rrow = E4(nc.sbuf_tensor("rrow", [1, 64], F32))
            u_tm2 = E4(nc.sbuf_tensor("u_tm2", [128, 4 * 2048], F32))
            u_sb = E4(nc.sbuf_tensor("u_sb", [128, 512], F32))
            wfo = [E4(nc.sbuf_tensor("wfo0", [128, 16 * 128], F32))] * 2
            wbo = [E4(nc.sbuf_tensor("wbo0", [128, 16 * 128], BF16))] * 2
            ares_st = [E4(nc.sbuf_tensor(f"ares_st{i}", [128, 512], F32)) for i in range(2)]
            sq_st = u_sb
            egl_bf = ares_st[0].ap().bitcast(BF16)[:, 0:512]
            J_s = E4(nc.sbuf_tensor("J_s", [64, 8], F32))
            JT_s = E4(nc.sbuf_tensor("JT_s", [8, 64], F32))
            coefm = E4(nc.sbuf_tensor("coefm", [8, 64], F32))
            S_core_bf = Eq_s.ap().bitcast(BF16)[:, 0:4096]

            @block.gpsimd
            def _(gpsimd):
                _w(gpsimd, t_gather)
                for q in range(8):
                    gq.inc(gpsimd.dma_start(out=J_s[:, q:q + 1], in_=E_all.ap()[q * 128:q * 128 + 64, 4096:4097], allow_slow_non_contiguous=True), 16)
            t_J = gq.tok()

            @block.tensor
            def _(tensor):
                _w(tensor, t_J)
                pe.inc(tensor.transpose(p_ms[0:8, 0:64], J_s[:, :], idnf_s[0:64, 0:64]))
            t_JT = pe.tok()

            @block.vector
            def _(vector):
                _w(vector, t_JT)
                dv.inc(vector.tensor_copy(out=JT_s[:, :], in_=p_ms[0:8, 0:64]))
            t_JTs = dv.tok()

            @block.tensor
            def _(tensor):
                _w(tensor, t_JTs)
                pe.inc(tensor.matmul(p_bc[0:8, 0:64], wmaskT_s[:, :], JT_s[:, :], start=True, stop=True))
            t_Lq = pe.tok()

            @block.scalar
            def _(scalar):
                _w(scalar, t_Lq)
                ac.inc(scalar.activation(out=coefm[:, :], in_=p_bc[0:8, 0:64], func=AF.Exp))
            t_cf1 = ac.tok()

            @block.vector
            def _(vector):
                _w(vector, t_cf1)
                dv.inc(vector.tensor_scalar(out=coefm[:, :], in0=coefm[:, :],
                                            scalar1=qmask_s[:, :], scalar2=None, op0=ALU.mult))
            t_cf2 = dv.tok()

            @block.gpsimd
            def _(gpsimd):
                _w(gpsimd, t_cf2)
                gq.inc(gpsimd.dma_start(out=coef_dram.ap(), in_=coefm[:, :]), 16)
                gpsimd.wait_ge(gq.sem, gq.n)
                gq.inc(gpsimd.dma_start(out=agrow[0:1, :],
                                        in_=coef_dram.ap().rearrange("q h -> (q h)").unsqueeze(0)), 16)
            t_cfl = gq.tok()

            @block.vector
            def _(vector):
                _w(vector, t_cfl)
                dv.inc(vector.memset(S_core[:, :], 0.0))
            t_cbe = dv.tok()

            eq_t = None
            for q in range(8):
                @block.gpsimd
                def _(gpsimd, q=q, eq_t=eq_t):
                    _w(gpsimd, t_gather, eq_t)
                    gq.inc(gpsimd.dma_start(out=Eq_s[:, :], in_=E_all.ap()[q * 128:(q + 1) * 128, 0:4096]), 16)
                t_eq = gq.tok()

                @block.tensor
                def _(tensor, q=q, eq_t=eq_t):
                    _w(tensor, t_cfl, eq_t)
                    pe.inc(tensor.matmul(p_bc[0:128, 0:64], ones_row[:, :],
                                         agrow[0:1, q * 64:(q + 1) * 64], start=True, stop=True))
                t_cq = pe.tok()

                @block.vector
                def _(vector, q=q, t_eq=t_eq, t_cq=t_cq):
                    _w(vector, t_eq, t_cbe, t_cq)
                    dv.inc(vector.tensor_tensor(
                        out=Eq_s.ap().rearrange("p (h w) -> p h w", h=64),
                        in0=Eq_s.ap().rearrange("p (h w) -> p h w", h=64),
                        in1=p_bc[0:128, 0:64, None].broadcast_to([128, 64, 64]), op=ALU.mult))
                    dv.inc(vector.tensor_tensor(out=S_core[:, :], in0=S_core[:, :], in1=Eq_s[:, :],
                                                op=ALU.add))
                eq_t = dv.tok()

            @block.gpsimd
            def _(gpsimd, eq_t=eq_t):
                _w(gpsimd, eq_t, t_ck)
                po.inc(gpsimd.tensor_copy(out=S_core_bf, in_=S_core[:, :]))
            t_scbf = po.tok()

            tile_done = None
            wodma_t = [None, None]
            wocast_t = [None, None]
            woused_t = [None, None]
            stat_t = None
            ares_free = [None, None]
            corr_free = [None, None]
            corr_free[0] = t_cbe
            corr_free[1] = t_JTs
            uacc_free = None
            for tt in range(2):
                @block.gpsimd
                def _(gpsimd, tt=tt, tile_done=tile_done):
                    _w(gpsimd, ydma_t[0], ydma_t[1], tile_done)
                    gq.inc(gpsimd.dma_start(out=ytile[:, :],
                                            in_=y_dram.ap()[:, tt * 16384:(tt + 1) * 16384]), 16)
                t_yt = gq.tok()

                @block.sync
                def _(sync, tt=tt, tile_done=tile_done):
                    _w(sync, tile_done)
                    sync.wait_ge(gq.sem, z_gq_done[1])
                    for g in range(32):
                        sq.inc(sync.dma_start(out=ztile[:, g * 512:(g + 1) * 512],
                                              in_=z_dram.ap()[:, g * T + tt * 512: g * T + tt * 512 + 512]), 16)
                t_zt = sq.tok()

                hcorr_t = eq_t
                hcg_t = None
                for h in range(NH):
                    par = h % 2
                    arow = 64 * par

                    @block.sync
                    def _(sync, tt=tt, h=h, arow=arow, hcorr_t=hcorr_t):
                        _w(sync, hcorr_t)
                        sq.inc(sync.dma_start(out=agrow[arow:arow + 1, :],
                                              in_=aglob_dram.ap()[h:h + 1, tt * 512:(tt + 1) * 512]), 16)
                    t_ag = sq.tok()

                    @block.tensor
                    def _(tensor, arow=arow, t_ag=t_ag):
                        _w(tensor, t_ag, corr_free[0])
                        pe.inc(tensor.matmul(p_bc[0:128, 0:512], ones65[arow:arow + 1, :],
                                             agrow[arow:arow + 1, :], start=True, stop=True))
                    t_agb = pe.tok()

                    @block.scalar
                    def _(scalar, t_agb=t_agb, hcg_t=hcg_t):
                        _w(scalar, t_agb, hcg_t)
                        ac.inc(scalar.activation(out=egl_bf[:, :], in_=p_bc[:, 0:512], func=AF.Exp))
                    t_egl = ac.tok()
                    corr_free[0] = t_egl

                    @block.vector
                    def _(vector, tt=tt, t_egl=t_egl, hcorr_t=hcorr_t):
                        _w(vector, t_egl, hcorr_t)
                        dv.inc(vector.tensor_tensor(out=Cg_bf[:, :], in0=egl_bf[:, :],
                                                    in1=C_keep[:, tt * 512:(tt + 1) * 512], op=ALU.mult))
                    hcg_t = dv.tok()

                    @block.tensor
                    def _(tensor, h=h, hcg_t=hcg_t):
                        _w(tensor, hcg_t, t_scbf, corr_free[1])
                        base = (h % 2) * 64
                        pe.inc(tensor.matmul(p_ms[base:base + 64, 0:512], S_core_bf[:, h * 64:(h + 1) * 64],
                                             Cg_bf[:, :], start=True, stop=True))
                    t_corr = pe.tok()

                    @block.vector
                    def _(vector, h=h, t_corr=t_corr, t_yt=t_yt):
                        _w(vector, t_corr, t_yt)
                        g = h // 2
                        base = (h % 2) * 64
                        yv = ytile[base:base + 64, :].rearrange("p (c gg t) -> p c gg t",
                                                                c=8, gg=32)[:, :, g, :]
                        ins = None
                        for cc_ in range(8):
                            yv2 = ytile[base:base + 64, cc_ * 2048 + g * 64: cc_ * 2048 + g * 64 + 64]
                            ins = vector.tensor_tensor(
                                out=yv2, in0=yv2,
                                in1=p_ms[base:base + 64, cc_ * 64:(cc_ + 1) * 64], op=ALU.add)
                        dv.inc(ins)
                    hcorr_t = dv.tok()
                    corr_free[1] = hcorr_t

                if tap == "ycorr" and tt == 0:
                    @block.gpsimd
                    def _(gpsimd):
                        finish(gpsimd, gq)
                        gq.inc(gpsimd.dma_start(out=dbg[:, 0:4096], in_=ytile[:, 0:4096]), 16)
                        gq.inc(gpsimd.dma_start(out=dbg[:, 4096:8192], in_=y_dram.ap()[:, 0:4096]), 16)
                        gpsimd.wait_ge(gq.sem, gq.n)
                    return nc

                if tap == "corr2" and tt == 1:
                    @block.gpsimd
                    def _(gpsimd):
                        finish(gpsimd, gq)
                        gq.inc(gpsimd.dma_start(out=dbg[:, 0:2048], in_=ytile[:, 0:2048]), 16)
                        gpsimd.wait_ge(gq.sem, gq.n)
                    return nc

                prev_rr = tile_done
                prev_rbe = tile_done
                prev_sq = tile_done
                for ci in range(8):
                    c = tt * 8 + ci

                    @block.vector
                    def _(vector, ci=ci, hcorr_t=hcorr_t, t_zt=t_zt):
                        _w(vector, hcorr_t, t_zt)
                        yv = ytile[:, ci * 2048:(ci + 1) * 2048].rearrange("p (g t) -> p g t", g=32)
                        zv = ztile.ap().rearrange("p (g t) -> p g t", g=32)[:, :, ci * 64:(ci + 1) * 64]
                        dv.inc(vector.tensor_tensor(out=yv, in0=yv, in1=zv, op=ALU.mult))
                    t_gate = dv.tok()

                    t_ss = None
                    for sub in range(4):
                        @block.scalar
                        def _(scalar, ci=ci, sub=sub, t_gate=t_gate, prev_sq=prev_sq, t_ss=t_ss):
                            _w(scalar, t_gate, prev_sq if sub == 0 else None, t_ss)
                            ac.inc(scalar.activation(
                                out=ysq_bf[:, :],
                                in_=ytile[:, ci * 2048 + sub * 512: ci * 2048 + sub * 512 + 512],
                                func=AF.Square))
                        t_ysq = ac.tok()

                        @block.tensor
                        def _(tensor, sub=sub, t_ysq=t_ysq, prev_rr=prev_rr):
                            _w(tensor, t_ysq, prev_rr if sub == 0 else None)
                            for g in range(8):
                                pe.inc(tensor.matmul(p_io0[0:1, 0:64], ones_col_b[:, :],
                                                     ysq_bf[:, g * 64:(g + 1) * 64],
                                                     start=(sub == 0 and g == 0), stop=(sub == 3 and g == 7)),
                                       1 if (sub == 3 and g == 7) else 0)
                        t_ss = pe.tok()
                    prev_sq = t_ss

                    @block.scalar
                    def _(scalar, t_ss=t_ss):
                        _w(scalar, t_ss)
                        ac.inc(scalar.activation(out=rrow[:, :], in_=p_io0[0:1, 0:64], func=AF.Ln,
                                                 scale=1.0 / DIN, bias=eps1[:, :]))
                        ac.inc(scalar.activation(out=rrow[:, :], in_=rrow[:, :], func=AF.Exp, scale=-0.5))
                    t_rr = ac.tok()
                    prev_rr = t_rr

                    @block.tensor
                    def _(tensor, t_rr=t_rr, prev_rbe=prev_rbe):
                        _w(tensor, t_rr, prev_rbe)
                        pe.inc(tensor.matmul(p_io1[0:128, 0:64], ones_row[:, :], rrow[:, :],
                                             start=True, stop=True))
                    t_rb = pe.tok()

                    @block.vector
                    def _(vector, t_rb=t_rb):
                        _w(vector, t_rb)
                        dv.inc(vector.tensor_copy(out=rstd_b[:, :], in_=p_io1[0:128, 0:64]))
                    t_rbe = dv.tok()
                    prev_rbe = t_rbe

                    @block.vector
                    def _(vector, ci=ci, t_rbe=t_rbe):
                        _w(vector, t_rbe)
                        yv = ytile[:, ci * 2048:(ci + 1) * 2048].rearrange("p (g t) -> p g t", g=32)
                        ov = ynorm_h.ap().rearrange("p (g tl) -> p g tl", g=32)[:, :, ci * 64:(ci + 1) * 64]
                        dv.inc(vector.tensor_tensor(out=yv, in0=yv,
                                                    in1=rstd_b[:, None, :].broadcast_to([128, 32, 64]),
                                                    op=ALU.mult))
                        dv.inc(vector.tensor_tensor(out=ov, in0=yv,
                                                    in1=nw_s[:, :, None].broadcast_to([128, 32, 64]),
                                                    op=ALU.mult))
                t_yn = dv.tok()

                if tap == "ynorm" and tt == 0:
                    @block.gpsimd
                    def _(gpsimd):
                        finish(gpsimd, gq)
                        gq.inc(gpsimd.dma_start(out=dbg[:, 0:2048], in_=ynorm_h[:, 0:2048]), 16)
                        gpsimd.wait_ge(gq.sem, gq.n)
                    return nc

                ares_free = [prev_rr, prev_rbe]
                @block.sync
                def _(sync, tt=tt, tile_done=tile_done):
                    _w(sync, tile_done)
                    sq.inc(sync.dma_start(out=u_tm2[:, :].rearrange("p (a d) -> p a d", a=4),
                                          in_=u_pad[3 + tt * 512: 3 + tt * 512 + 512, :].rearrange(
                                              "(a p) d -> p a d", p=128)), 16)
                t_u2 = sq.tok()

                for m in range(16):
                    par = m % 2
                    bank = [p_io0, p_io1][par]

                    @block.sync
                    def _(sync, m=m):
                        _w(sync, wocast_t[0])
                        src = WoutT.rearrange("(kc kp) mm -> kp kc mm", kp=128)[:, 0:16, m * 128:(m + 1) * 128]
                        dst = wfo[0][:, :].rearrange("p (kc mm) -> p kc mm", kc=16)
                        sq.inc(sync.dma_start(out=dst, in_=src), 16)
                    wodma_t[0] = sq.tok()

                    for kk in range(2):
                        if kk == 1:
                            @block.sync
                            def _(sync, m=m):
                                _w(sync, wocast_t[0])
                                src = WoutT.rearrange("(kc kp) mm -> kp kc mm",
                                                      kp=128)[:, 16:32, m * 128:(m + 1) * 128]
                                dst = wfo[0][:, :].rearrange("p (kc mm) -> p kc mm", kc=16)
                                sq.inc(sync.dma_start(out=dst, in_=src), 16)
                            wodma_t[0] = sq.tok()

                        @block.gpsimd
                        def _(gpsimd):
                            _w(gpsimd, wodma_t[0], woused_t[0])
                            po.inc(gpsimd.tensor_copy(out=wbo[0][:, :], in_=wfo[0][:, :]))
                        wocast_t[0] = po.tok()

                        @block.tensor
                        def _(tensor, m=m, par=par, kk=kk, bank=bank, t_yn=t_yn):
                            _w(tensor, wocast_t[0], t_yn)
                            if kk == 0:
                                _w(tensor, ares_free[par])
                            for k in range(16):
                                pe.inc(tensor.matmul(bank[:, 0:512], wbo[0][:, k * 128:(k + 1) * 128],
                                                     ynorm_h[:, (kk * 16 + k) * 512: (kk * 16 + k) * 512 + 512],
                                                     start=(kk == 0 and k == 0), stop=(kk == 1 and k == 15)),
                                       1 if k == 15 else 0)
                        woused_t[0] = pe.tok()
                    t_wo = pe.tok()

                    @block.tensor
                    def _(tensor, m=m, t_u2=t_u2, uacc_free=uacc_free):
                        _w(tensor, t_u2, uacc_free)
                        for j in range(4):
                            pe.inc(tensor.transpose(p_y0[0:128, j * 128:(j + 1) * 128],
                                                    u_tm2[:, j * 2048 + m * 128: j * 2048 + m * 128 + 128],
                                                    idnf_s[:, :]))
                    t_utp = pe.tok()

                    @block.scalar
                    def _(scalar, t_utp=t_utp, stat_t=stat_t):
                        _w(scalar, t_utp, stat_t)
                        ac.inc(scalar.activation(out=u_sb[:, :], in_=p_y0[:, 0:512], func=AF.Copy))
                    t_usb = ac.tok()
                    uacc_free = t_usb

                    @block.vector
                    def _(vector, m=m, par=par, bank=bank, t_wo=t_wo, t_usb=t_usb):
                        _w(vector, t_wo, t_usb, ares_free[par])
                        dv.inc(vector.tensor_tensor(out=ares_st[par][:, :], in0=bank[:, 0:512],
                                                    in1=u_sb[:, :], op=ALU.add))
                    t_ares = dv.tok()

                    @block.gpsimd
                    def _(gpsimd, m=m, tt=tt, par=par, t_ares=t_ares):
                        _w(gpsimd, t_ares)
                        gq.inc(gpsimd.dma_start(
                            out=ares_dram.ap()[:, m * T + tt * 512: m * T + tt * 512 + 512],
                            in_=ares_st[par][:, :]), 16)
                    t_adma = gq.tok()

                    @block.scalar
                    def _(scalar, par=par, t_ares=t_ares, stat_t=stat_t):
                        _w(scalar, t_ares, stat_t)
                        ac.inc(scalar.activation(out=sq_st[:, :], in_=ares_st[par][:, :], func=AF.Square))
                    t_sqs = ac.tok()

                    @block.tensor
                    def _(tensor, par=par, t_sqs=t_sqs, stat_t=stat_t):
                        _w(tensor, t_sqs, stat_t)
                        pe.inc(tensor.matmul(p_ms[0:1, 0:512], ones_col[:, :], ares_st[par][:, :],
                                             start=True, stop=True))
                        pe.inc(tensor.matmul(p_bc[0:1, 0:512], ones_col[:, :], sq_st[:, :],
                                             start=True, stop=True))
                    t_st = pe.tok()

                    @block.vector
                    def _(vector, m=m, tt=tt, t_st=t_st, t_adma=t_adma):
                        _w(vector, t_st, t_adma)
                        if m == 0:
                            dv.inc(vector.tensor_copy(out=stat_row[0:1, tt * 512:(tt + 1) * 512],
                                                      in_=p_ms[0:1, 0:512]))
                            dv.inc(vector.tensor_copy(out=stat_row2[0:1, tt * 512:(tt + 1) * 512],
                                                      in_=p_bc[0:1, 0:512]))
                        else:
                            dv.inc(vector.tensor_tensor(out=stat_row[0:1, tt * 512:(tt + 1) * 512],
                                                        in0=stat_row[0:1, tt * 512:(tt + 1) * 512],
                                                        in1=p_ms[0:1, 0:512], op=ALU.add))
                            dv.inc(vector.tensor_tensor(out=stat_row2[0:1, tt * 512:(tt + 1) * 512],
                                                        in0=stat_row2[0:1, tt * 512:(tt + 1) * 512],
                                                        in1=p_bc[0:1, 0:512], op=ALU.add))
                    stat_t = dv.tok()
                    ares_free[par] = stat_t
                    corr_free[0] = stat_t
                    corr_free[1] = stat_t
                tile_done = dv.tok()
                if tap == "wout2" and tt == 1:
                    @block.gpsimd
                    def _(gpsimd):
                        finish(gpsimd, gq)
                        gq.inc(gpsimd.dma_start(out=dbg[:, 0:1024], in_=ares_dram.ap()[:, 0:1024]), 16)
                        gpsimd.wait_ge(gq.sem, gq.n)
                    return nc
                if tap == "wout" and tt == 0:
                    @block.gpsimd
                    def _(gpsimd):
                        finish(gpsimd, gq)
                        gq.inc(gpsimd.dma_start(out=dbg[:, 0:1024], in_=ares_dram.ap()[:, 0:1024]), 16)
                        gq.inc(gpsimd.dma_start(out=dbg[:, 1024:2048], in_=ares_dram.ap()[:, 1024:2048]), 16)
                        gpsimd.wait_ge(gq.sem, gq.n)
                    return nc
            t_p5 = dv.tok()

            @block.vector
            def _(vector):
                _w(vector, t_p5)
                dv.inc(vector.tensor_scalar(out=stat_row[0:1, :], in0=stat_row[0:1, :],
                                            scalar1=1.0 / D, scalar2=None, op0=ALU.mult))
                dv.inc(vector.tensor_scalar(out=stat_row2[0:1, :], in0=stat_row2[0:1, :],
                                            scalar1=1.0 / D, scalar2=None, op0=ALU.mult))
                dv.inc(vector.tensor_tensor(out=ln_scr[0:1, :], in0=stat_row[0:1, :],
                                            in1=stat_row[0:1, :], op=ALU.mult))
                dv.inc(vector.tensor_tensor(out=stat_row2[0:1, :], in0=stat_row2[0:1, :],
                                            in1=ln_scr[0:1, :], op=ALU.subtract))
            t_v1 = dv.tok()
            if tap == "p5a":
                @block.gpsimd
                def _(gpsimd):
                    finish(gpsimd, gq)
                    gq.inc(gpsimd.dma_start(out=dbg[0:1, 0:1024], in_=stat_row[0:1, :]), 16)
                    gq.inc(gpsimd.dma_start(out=dbg[0:1, 1024:2048], in_=stat_row2[0:1, :]), 16)
                    gpsimd.wait_ge(gq.sem, gq.n)
                return nc

            @block.scalar
            def _(scalar):
                _w(scalar, t_v1)
                ac.inc(scalar.activation(out=stat_row2[0:1, :], in_=stat_row2[0:1, :], func=AF.Ln,
                                         bias=eps1[:, :]))
                ac.inc(scalar.activation(out=stat_row2[0:1, :], in_=stat_row2[0:1, :], func=AF.Exp,
                                         scale=-0.5))
            t_r1 = ac.tok()
            if tap == "p5b":
                @block.gpsimd
                def _(gpsimd):
                    finish(gpsimd, gq)
                    gq.inc(gpsimd.dma_start(out=dbg[0:1, 0:1024], in_=stat_row[0:1, :]), 16)
                    gq.inc(gpsimd.dma_start(out=dbg[0:1, 1024:2048], in_=stat_row2[0:1, :]), 16)
                    gpsimd.wait_ge(gq.sem, gq.n)
                return nc

            @block.vector
            def _(vector):
                _w(vector, t_r1)
                dv.inc(vector.tensor_tensor(out=stat_row[0:1, :], in0=stat_row[0:1, :],
                                            in1=stat_row2[0:1, :], op=ALU.mult))
                dv.inc(vector.tensor_scalar(out=stat_row[0:1, :], in0=stat_row[0:1, :],
                                            scalar1=-1.0, scalar2=None, op0=ALU.mult))
            t_am = dv.tok()

            if tap == "p5c":
                @block.gpsimd
                def _(gpsimd):
                    finish(gpsimd, gq)
                    gq.inc(gpsimd.dma_start(out=dbg[0:1, 0:1024], in_=stat_row[0:1, :]), 16)
                    gpsimd.wait_ge(gq.sem, gq.n)
                return nc

            bc_prev = None
            for half in range(2):
                @block.tensor
                def _(tensor, half=half, bc_prev=bc_prev):
                    _w(tensor, t_am, bc_prev)
                    pe.inc(tensor.matmul(p_io0[:, 0:512], ones_row[:, :],
                                         stat_row2[0:1, half * 512:(half + 1) * 512], start=True, stop=True))
                    pe.inc(tensor.matmul(p_io1[:, 0:512], ones_row[:, :],
                                         stat_row[0:1, half * 512:(half + 1) * 512], start=True, stop=True))
                t_b2 = pe.tok()

                @block.vector
                def _(vector, half=half, t_b2=t_b2):
                    _w(vector, t_b2)
                    dv.inc(vector.tensor_copy(out=rstd1_b[:, half * 512:(half + 1) * 512],
                                              in_=p_io0[:, 0:512]))
                    dv.inc(vector.tensor_copy(out=am1_b[:, half * 512:(half + 1) * 512],
                                              in_=p_io1[:, 0:512]))
                bc_prev = dv.tok()
                if tap == "p5d" and half == 0:
                    @block.gpsimd
                    def _(gpsimd):
                        finish(gpsimd, gq)
                        gq.inc(gpsimd.dma_start(out=dbg[:, 0:1024], in_=rstd1_b[:, :]), 16)
                        gpsimd.wait_ge(gq.sem, gq.n)
                    return nc
            t_ln1 = dv.tok()
            if tap == "p5":
                @block.gpsimd
                def _(gpsimd):
                    finish(gpsimd, gq)
                    gq.inc(gpsimd.dma_start(out=dbg[:, 0:1024], in_=rstd1_b[:, :]), 16)
                    gq.inc(gpsimd.dma_start(out=dbg[:, 1024:2048], in_=am1_b[:, :]), 16)
                    gpsimd.wait_ge(gq.sem, gq.n)
                return nc
            barrier()

        with ExitStack() as R56:
            E56 = R56.enter_context
            g_res = E56(nc.sbuf_tensor("g_res", [128, 43 * T], BF16))
            ar_ld = [E56(nc.sbuf_tensor(f"ar_ld{i}", [128, T], F32)) for i in range(2)]

            with ExitStack() as R5:
                E5 = R5.enter_context
                x1bf = E5(nc.sbuf_tensor("x1bf", [128, 16 * T], BF16))
                wq = [E5(nc.sbuf_tensor(f"wq{i}", [128, 2048], F32)) for i in range(4)]
                q1_bf = [E5(nc.sbuf_tensor(f"q1_bf{i}", [128, 2048], BF16)) for i in range(2)]
                q3_bf = [E5(nc.sbuf_tensor(f"q3_bf{i}", [128, 2048], BF16)) for i in range(2)]
                qs_bf = [E5(nc.sbuf_tensor("qs_bf0", [128, 2048], BF16))] * 2
                sh1_bf = E5(nc.sbuf_tensor("sh1_bf", [128, 512], BF16))

                ar_t = [None, None]
                x1_t = [None, None]
                for k in range(16):
                    par = k % 2

                    @block.sync
                    def _(sync, k=k, par=par):
                        _w(sync, x1_t[par])
                        sq.inc(sync.dma_start(out=ar_ld[par][:, :],
                                              in_=ares_dram.ap()[:, k * T:(k + 1) * T]), 16)
                    ar_t[par] = sq.tok()

                    @block.vector
                    def _(vector, k=k, par=par):
                        _w(vector, ar_t[par], t_ln1)
                        dv.inc(vector.tensor_tensor(out=ar_ld[par][:, :], in0=ar_ld[par][:, :],
                                                    in1=rstd1_b[:, :], op=ALU.mult))
                        dv.inc(vector.tensor_tensor(out=x1bf[:, k * T:(k + 1) * T], in0=ar_ld[par][:, :],
                                                    in1=am1_b[:, :], op=ALU.add))
                    x1_t[par] = dv.tok()
                t_x1 = dv.tok()

                if tap == "x1":
                    @block.gpsimd
                    def _(gpsimd):
                        finish(gpsimd, gq)
                        gq.inc(gpsimd.dma_start(out=dbg[:, 0:1024], in_=x1bf[:, 0:1024]), 16)
                        gq.inc(gpsimd.dma_start(out=dbg[:, 1024:2048], in_=x1bf[:, 1024:2048]), 16)
                        gpsimd.wait_ge(gq.sem, gq.n)
                    return nc

                wdma2_t = [None, None]
                q_t = [None, None]
                qused_t = [None, None]
                hio_free = [None, None, None, None]
                HB = [p_io0, p_io1, p_bc, p_ms]
                g_t = None
                for m in range(43):
                    par = m % 2
                    mw = 128 if m < 42 else 85

                    @block.sync
                    def _(sync, m=m, par=par, mw=mw):
                        _w(sync, q_t[par])
                        for wsrc, slot in ((w1T, 0), (w3T, 1)):
                            src = wsrc.rearrange("(kc kp) mm -> kp kc mm", kp=128)[:, :, m * 128:m * 128 + mw]
                            dst = wq[slot * 2 + par][:, 0:16 * mw].rearrange("p (kc mm) -> p kc mm", kc=16)
                            sq.inc(sync.dma_start(out=dst, in_=src), 16)
                    wdma2_t[par] = sq.tok()

                    @block.scalar
                    def _(scalar, par=par, mw=mw):
                        _w(scalar, wdma2_t[par], t_shalf, qused_t[par])
                        ac.inc(scalar.activation(out=q1_bf[par][:, 0:16 * mw], in_=wq[par][:, 0:16 * mw],
                                                 func=AF.Sign, bias=sneg_s[:, 0:1]))
                        ac.inc(scalar.activation(out=qs_bf[par][:, 0:16 * mw], in_=wq[par][:, 0:16 * mw],
                                                 func=AF.Sign, bias=sneg_s[:, 0:1], scale=-1.0))
                    t_sg1 = ac.tok()

                    @block.vector
                    def _(vector, par=par, mw=mw, t_sg1=t_sg1):
                        _w(vector, t_sg1)
                        dv.inc(vector.tensor_tensor(out=q1_bf[par][:, 0:16 * mw],
                                                    in0=q1_bf[par][:, 0:16 * mw],
                                                    in1=qs_bf[par][:, 0:16 * mw], op=ALU.subtract))
                    t_q1 = dv.tok()

                    @block.scalar
                    def _(scalar, par=par, mw=mw, t_q1=t_q1):
                        _w(scalar, t_q1)
                        ac.inc(scalar.activation(out=qs_bf[par][:, 0:16 * mw], in_=wq[2 + par][:, 0:16 * mw],
                                                 func=AF.Sign, bias=sneg_s[:, 2:3]))
                        ac.inc(scalar.activation(out=q3_bf[par][:, 0:16 * mw], in_=wq[2 + par][:, 0:16 * mw],
                                                 func=AF.Sign, bias=sneg_s[:, 2:3], scale=-1.0))
                    t_sg3 = ac.tok()

                    @block.vector
                    def _(vector, par=par, mw=mw, t_sg3=t_sg3):
                        _w(vector, t_sg3)
                        dv.inc(vector.tensor_tensor(out=q3_bf[par][:, 0:16 * mw],
                                                    in0=qs_bf[par][:, 0:16 * mw],
                                                    in1=q3_bf[par][:, 0:16 * mw], op=ALU.subtract))
                    q_t[par] = dv.tok()

                    for nt in range(2):
                        b1 = HB[nt * 2]
                        b3 = HB[nt * 2 + 1]

                        @block.tensor
                        def _(tensor, m=m, par=par, mw=mw, nt=nt, b1=b1, b3=b3):
                            _w(tensor, q_t[par], t_x1, hio_free[nt * 2], hio_free[nt * 2 + 1])
                            for k in range(16):
                                pe.inc(tensor.matmul(b1[0:mw, 0:512], q1_bf[par][:, k * mw: k * mw + mw],
                                                     x1bf[:, k * T + nt * 512: k * T + nt * 512 + 512],
                                                     start=(k == 0), stop=(k == 15)), 1 if k == 15 else 0)
                            for k in range(16):
                                pe.inc(tensor.matmul(b3[0:mw, 0:512], q3_bf[par][:, k * mw: k * mw + mw],
                                                     x1bf[:, k * T + nt * 512: k * T + nt * 512 + 512],
                                                     start=(k == 0), stop=(k == 15)), 1 if k == 15 else 0)
                        t_h = pe.tok()
                        qused_t[par] = t_h

                        @block.scalar
                        def _(scalar, mw=mw, b1=b1, t_h=t_h, g_t=g_t):
                            _w(scalar, t_h, g_t)
                            ac.inc(scalar.activation(out=sh1_bf[0:mw, :], in_=b1[0:mw, 0:512],
                                                     func=AF.Silu, scale=shalf_s[0:mw, 0:1]))
                        t_sh1 = ac.tok()

                        @block.vector
                        def _(vector, m=m, mw=mw, nt=nt, b3=b3, t_sh1=t_sh1):
                            _w(vector, t_sh1)
                            dv.inc(vector.scalar_tensor_tensor(
                                out=g_res[0:mw, m * T + nt * 512: m * T + nt * 512 + 512],
                                in0=b3[0:mw, 0:512], scalar=shalf_s[0:mw, 2:3], in1=sh1_bf[0:mw, :],
                                op0=ALU.mult, op1=ALU.mult))
                        g_t = dv.tok()
                        hio_free[nt * 2] = g_t
                        hio_free[nt * 2 + 1] = g_t
                t_g = dv.tok()

                if tap == "g":
                    @block.gpsimd
                    def _(gpsimd):
                        finish(gpsimd, gq)
                        gq.inc(gpsimd.dma_start(out=dbg[:, 0:1024], in_=g_res[:, 0:1024]), 16)
                        gq.inc(gpsimd.dma_start(out=dbg[:, 1024:2048], in_=g_res[:, 42 * T:42 * T + 1024]), 16)
                        gpsimd.wait_ge(gq.sem, gq.n)
                    return nc
                barrier()

            with ExitStack() as R6:
                E6 = R6.enter_context
                wq2 = E6(nc.sbuf_tensor("w2buf", [128, 43 * 128], F32))
                q2_bf = E6(nc.sbuf_tensor("q2bbuf", [128, 43 * 128], BF16))
                qs2_bf = E6(nc.sbuf_tensor("qs2buf", [128, 43 * 128], BF16))
                x1f_st = E6(nc.sbuf_tensor("x1f_st", [128, T], F32))
                x2_st = [E6(nc.sbuf_tensor(f"x2_st{i}", [128, 512], F32)) for i in range(2)]
                sq2_st = E6(nc.sbuf_tensor("sq2_st", [128, 512], F32))

                w2dma_t = None
                q2_t = None
                q2used_t = None
                ar2_t = [None, None]
                x2_free = [None, None]
                st2_t = None
                for m in range(16):
                    par = m % 2

                    @block.sync
                    def _(sync, m=m, par=par, q2_t=q2_t):
                        _w(sync, q2_t)
                        if m < 2:
                            _w(sync, t_g)
                        src = w2T[0:5376, m * 128:(m + 1) * 128].rearrange("(kc kp) mm -> kp kc mm", kp=128)
                        dst = wq2[:, 0:42 * 128].rearrange("p (kc mm) -> p kc mm", kc=42)
                        sq.inc(sync.dma_start(out=dst, in_=src), 16)
                        sq.inc(sync.dma_start(out=wq2[0:85, 42 * 128:43 * 128],
                                              in_=w2T[5376:5461, m * 128:(m + 1) * 128]), 16)
                        _w(sync, ar2_t[par])
                        sq.inc(sync.dma_start(out=ar_ld[par][:, :],
                                              in_=ares_dram.ap()[:, m * T:(m + 1) * T]), 16)
                    w2dma_t = sq.tok()

                    @block.scalar
                    def _(scalar, w2dma_t=w2dma_t, q2used_t=q2used_t):
                        _w(scalar, w2dma_t, q2used_t)
                        ac.inc(scalar.activation(out=q2_bf[:, :], in_=wq2[:, :],
                                                 func=AF.Sign, bias=sneg_s[:, 1:2]))
                        ac.inc(scalar.activation(out=qs2_bf[:, :], in_=wq2[:, :],
                                                 func=AF.Sign, bias=sneg_s[:, 1:2], scale=-1.0))
                    t_sg2 = ac.tok()

                    @block.vector
                    def _(vector, par=par, t_sg2=t_sg2):
                        _w(vector, t_sg2)
                        dv.inc(vector.tensor_tensor(out=q2_bf[:, :], in0=q2_bf[:, :],
                                                    in1=qs2_bf[:, :], op=ALU.subtract))
                        dv.inc(vector.tensor_tensor(out=ar_ld[par][:, :], in0=ar_ld[par][:, :],
                                                    in1=rstd1_b[:, :], op=ALU.mult))
                        dv.inc(vector.tensor_tensor(out=x1f_st[:, :], in0=ar_ld[par][:, :],
                                                    in1=am1_b[:, :], op=ALU.add))
                    q2_t = dv.tok()
                    ar2_t[par] = q2_t

                    for nt in range(2):
                        bank = [p_io0, p_io1][nt]

                        @block.tensor
                        def _(tensor, nt=nt, bank=bank, q2_t=q2_t):
                            _w(tensor, q2_t, t_g, x2_free[nt])
                            for k in range(43):
                                kw = 128 if k < 42 else 85
                                pe.inc(tensor.matmul(bank[:, 0:512], q2_bf[0:kw, k * 128:k * 128 + 128],
                                                     g_res[0:kw, k * T + nt * 512: k * T + nt * 512 + 512],
                                                     start=(k == 0), stop=(k == 42)), 1 if k == 42 else 0)
                        t_o = pe.tok()
                        q2used_t = t_o

                        @block.vector
                        def _(vector, nt=nt, bank=bank, t_o=t_o, q2_t=q2_t):
                            _w(vector, t_o)
                            dv.inc(vector.scalar_tensor_tensor(
                                out=x2_st[nt][:, :],
                                in0=bank[:, 0:512], scalar=shalf_s[:, 1:2],
                                in1=x1f_st[:, nt * 512:(nt + 1) * 512],
                                op0=ALU.mult, op1=ALU.add))
                        t_x2p = dv.tok()

                        @block.gpsimd
                        def _(gpsimd, m=m, nt=nt, t_x2p=t_x2p):
                            _w(gpsimd, t_x2p)
                            gq.inc(gpsimd.dma_start(
                                out=x2_dram.ap()[:, m * T + nt * 512: m * T + nt * 512 + 512],
                                in_=x2_st[nt][:, :]), 16)
                        t_x2dma = gq.tok()

                        @block.scalar
                        def _(scalar, nt=nt, t_x2p=t_x2p, st2_t=st2_t):
                            _w(scalar, t_x2p, st2_t)
                            ac.inc(scalar.activation(out=sq2_st[:, :], in_=x2_st[nt][:, :], func=AF.Square))
                        t_sq2 = ac.tok()

                        @block.tensor
                        def _(tensor, nt=nt, t_sq2=t_sq2, st2_t=st2_t):
                            _w(tensor, t_sq2, st2_t)
                            pe.inc(tensor.matmul(p_ms[0:1, 0:512], ones_col[:, :], x2_st[nt][:, :],
                                                 start=True, stop=True))
                            pe.inc(tensor.matmul(p_bc[0:1, 0:512], ones_col[:, :], sq2_st[:, :],
                                                 start=True, stop=True))
                        t_st2 = pe.tok()

                        @block.vector
                        def _(vector, m=m, nt=nt, t_st2=t_st2, t_x2dma=t_x2dma):
                            _w(vector, t_st2, t_x2dma)
                            tc = nt * 512
                            if m == 0:
                                dv.inc(vector.tensor_copy(out=stat_row[0:1, tc:tc + 512],
                                                          in_=p_ms[0:1, 0:512]))
                                dv.inc(vector.tensor_copy(out=stat_row2[0:1, tc:tc + 512],
                                                          in_=p_bc[0:1, 0:512]))
                            else:
                                dv.inc(vector.tensor_tensor(out=stat_row[0:1, tc:tc + 512],
                                                            in0=stat_row[0:1, tc:tc + 512],
                                                            in1=p_ms[0:1, 0:512], op=ALU.add))
                                dv.inc(vector.tensor_tensor(out=stat_row2[0:1, tc:tc + 512],
                                                            in0=stat_row2[0:1, tc:tc + 512],
                                                            in1=p_bc[0:1, 0:512], op=ALU.add))
                        st2_t = dv.tok()
                        x2_free[nt] = st2_t
                t_w2done = dv.tok()
                barrier()

            @block.vector
            def _(vector):
                _w(vector, t_w2done)
                dv.inc(vector.tensor_scalar(out=stat_row[0:1, :], in0=stat_row[0:1, :],
                                            scalar1=1.0 / D, scalar2=None, op0=ALU.mult))
                dv.inc(vector.tensor_scalar(out=stat_row2[0:1, :], in0=stat_row2[0:1, :],
                                            scalar1=1.0 / D, scalar2=None, op0=ALU.mult))
                dv.inc(vector.tensor_tensor(out=ln_scr[0:1, :], in0=stat_row[0:1, :],
                                            in1=stat_row[0:1, :], op=ALU.mult))
                dv.inc(vector.tensor_tensor(out=stat_row2[0:1, :], in0=stat_row2[0:1, :],
                                            in1=ln_scr[0:1, :], op=ALU.subtract))
            t_v2 = dv.tok()

            @block.scalar
            def _(scalar):
                _w(scalar, t_v2)
                ac.inc(scalar.activation(out=stat_row2[0:1, :], in_=stat_row2[0:1, :], func=AF.Ln,
                                         bias=eps1[:, :]))
                ac.inc(scalar.activation(out=stat_row2[0:1, :], in_=stat_row2[0:1, :], func=AF.Exp,
                                         scale=-0.5))
            t_r2 = ac.tok()

            @block.vector
            def _(vector):
                _w(vector, t_r2)
                dv.inc(vector.tensor_tensor(out=stat_row[0:1, :], in0=stat_row[0:1, :],
                                            in1=stat_row2[0:1, :], op=ALU.mult))
                dv.inc(vector.tensor_scalar(out=stat_row[0:1, :], in0=stat_row[0:1, :],
                                            scalar1=-1.0, scalar2=None, op0=ALU.mult))
            t_am2 = dv.tok()

            bc2_prev = None
            for half in range(2):
                @block.tensor
                def _(tensor, half=half, bc2_prev=bc2_prev):
                    _w(tensor, t_am2, bc2_prev)
                    pe.inc(tensor.matmul(p_io0[:, 0:512], ones_row[:, :],
                                         stat_row2[0:1, half * 512:(half + 1) * 512], start=True, stop=True))
                    pe.inc(tensor.matmul(p_io1[:, 0:512], ones_row[:, :],
                                         stat_row[0:1, half * 512:(half + 1) * 512], start=True, stop=True))
                t_b3 = pe.tok()

                @block.vector
                def _(vector, half=half, t_b3=t_b3):
                    _w(vector, t_b3)
                    dv.inc(vector.tensor_copy(out=rstd1_b[:, half * 512:(half + 1) * 512],
                                              in_=p_io0[:, 0:512]))
                    dv.inc(vector.tensor_copy(out=am1_b[:, half * 512:(half + 1) * 512],
                                              in_=p_io1[:, 0:512]))
                bc2_prev = dv.tok()
            t_ln2 = dv.tok()

            pass
        barrier()

        with ExitStack() as R7:
            E7 = R7.enter_context
            x2ld = [E7(nc.sbuf_tensor(f"x2ld{i}", [128, T], F32)) for i in range(2)]
            out_all = E7(nc.sbuf_tensor("out_all", [128, 16384], F32))
            x2ld_t = [None, None]
            x2n_t = [None, None]
            otp_free = [None, None]
            last_ot = None
            for m in range(16):
                par = m % 2

                @block.sync
                def _(sync, m=m, par=par):
                    _w(sync, x2n_t[par])
                    sq.inc(sync.dma_start(out=x2ld[par][:, :], in_=x2_dram.ap()[:, m * T:(m + 1) * T]), 16)
                x2ld_t[par] = sq.tok()

                @block.vector
                def _(vector, m=m, par=par):
                    _w(vector, x2ld_t[par], t_ln2)
                    dv.inc(vector.tensor_tensor(out=x2ld[par][:, :], in0=x2ld[par][:, :],
                                                in1=rstd1_b[:, :], op=ALU.mult))
                    dv.inc(vector.tensor_tensor(out=x2ld[par][:, :], in0=x2ld[par][:, :],
                                                in1=am1_b[:, :], op=ALU.add))
                t_xn = dv.tok()
                x2n_t[par] = t_xn

                for grp in range(2):
                    bank = [p_y0, p_y1][grp]

                    @block.tensor
                    def _(tensor, m=m, par=par, grp=grp, bank=bank, t_xn=t_xn):
                        _w(tensor, t_xn, otp_free[grp])
                        for j in range(4):
                            i = grp * 4 + j
                            pe.inc(tensor.transpose(bank[0:128, j * 128:(j + 1) * 128],
                                                    x2ld[par][:, i * 128:(i + 1) * 128],
                                                    idnf_s[:, :]))
                    t_otp = pe.tok()

                    @block.vector
                    def _(vector, m=m, grp=grp, bank=bank, t_otp=t_otp):
                        _w(vector, t_otp)
                        for j in range(4):
                            i = grp * 4 + j
                            dv.inc(vector.tensor_copy(out=out_all[:, i * 2048 + m * 128: i * 2048 + m * 128 + 128],
                                                      in_=bank[0:128, j * 128:(j + 1) * 128]))
                    otp_free[grp] = dv.tok()
                x2n_t[par] = dv.tok()
            t_oall = dv.tok()

            @block.sync
            def _(sync):
                _w(sync, t_oall)
                for i in range(8):
                    sq.inc(sync.dma_start(out=out_d[i * 128:(i + 1) * 128, :],
                                          in_=out_all[:, i * 2048:(i + 1) * 2048]), 16)
                sync.wait_ge(sq.sem, sq.n)
                for e in ENGS:
                    if e.n and e is not sq:
                        sync.wait_ge(e.sem, e.n)
    return nc


def _prep_inputs(inputs):
    import ml_dtypes
    u = np.asarray(inputs["L0_attn_u"])[0]
    W_in = np.asarray(inputs["L0_attn_W_in"])
    conv_w = np.asarray(inputs["L0_attn_conv_w"])
    conv_b = np.asarray(inputs["L0_attn_conv_b"])
    dt_bias = np.asarray(inputs["L0_attn_dt_bias"])
    A_log = np.asarray(inputs["L0_attn_A_log"])
    D_param = np.asarray(inputs["L0_attn_D_param"])
    norm_w = np.asarray(inputs["L0_attn_norm_w"])
    W_out = np.asarray(inputs["L0_attn_W_out"])
    w1 = np.asarray(inputs["L0_ffn_w1"])
    w3 = np.asarray(inputs["L0_ffn_w3"])
    w2 = np.asarray(inputs["L0_ffn_w2"])

    WinT = np.ascontiguousarray(W_in.T)
    WoutT = np.ascontiguousarray(W_out.T)
    w1T = np.ascontiguousarray(w1.T)
    w3T = np.ascontiguousarray(w3.T)
    w2T = np.ascontiguousarray(w2.T)

    cw = conv_w[:, 0, :]
    taps = np.concatenate([cw[:, k].reshape(34, 128).T for k in range(4)], axis=1)
    conv_b_cm = np.ascontiguousarray(conv_b.reshape(34, 128).T)
    D_cm = np.ascontiguousarray(np.repeat(D_param, 64).reshape(32, 128).T)
    normw_cm = np.ascontiguousarray(norm_w.reshape(32, 128).T)
    idnf_a = np.eye(128, dtype=np.float32)
    idnb_a = np.eye(128, dtype=np.float32).astype(ml_dtypes.bfloat16)
    s_idx = np.arange(64)[:, None]
    l_idx = np.arange(64)[None, :]
    m64 = np.where(s_idx <= l_idx, 0.0, -1e30).astype(np.float32)
    maskneg = m64.astype(ml_dtypes.bfloat16)

    def wsl_for(r):
        parts = []
        for w in (w1, w2, w3):
            flat = w.reshape(-1)
            n8 = flat.size // NCORE
            sl = flat[r * n8:(r + 1) * n8].reshape(128, -1)
            pad = np.zeros((128, SLICE_W - sl.shape[1]), np.float32)
            parts.append(np.concatenate([sl, pad], axis=1))
        return np.ascontiguousarray(np.concatenate(parts, axis=1))

    shared = dict(
        WinT=WinT, WoutT=WoutT, w1T=w1T, w3T=w3T, w2T=w2T,
        conv_taps=np.ascontiguousarray(taps), conv_b_cm=conv_b_cm, D_cm=D_cm,
        normw_cm=normw_cm, dt_bias_c=np.ascontiguousarray(dt_bias.reshape(64, 1)),
        A_log_c=np.ascontiguousarray(A_log.reshape(64, 1)),
        idnf=idnf_a, idnb=idnb_a, masknegd=np.ascontiguousarray(maskneg),
    )
    in_maps = []
    for r in range(NCORE):
        up = np.zeros((HC, D), np.float32)
        up[3:] = u[r * T:(r + 1) * T]
        if r > 0:
            up[0:3] = u[r * T - 3:r * T]
        wmT = np.zeros((8, 8), np.float32)
        for j in range(8):
            for q in range(8):
                wmT[j, q] = 1.0 if (q < j < r) else 0.0
        qm = (np.arange(8) < r).astype(np.float32).reshape(8, 1)
        m = dict(shared)
        m.update(u_pad=np.ascontiguousarray(up), wslice=wsl_for(r), wmaskT=wmT, qmask=qm)
        in_maps.append(m)
    return in_maps


_NC_CACHE = {}


def _get_nc(tap=None):
    if tap not in _NC_CACHE:
        _NC_CACHE[tap] = build(tap)
    return _NC_CACHE[tap]


def kernel(**inputs):
    in_maps = _prep_inputs(inputs)
    nc = _get_nc(None)
    res = run_bass_kernel_spmd(nc, in_maps, core_ids=list(range(NCORE)))
    outs = [np.asarray(res.results[r]["out_d"], np.float32) for r in range(NCORE)]
    return np.concatenate(outs, axis=0)[None, :, :]


def run_tap(tap, inputs):
    in_maps = _prep_inputs(inputs)
    nc = _get_nc(tap)
    res = run_bass_kernel_spmd(nc, in_maps, core_ids=list(range(NCORE)))
    return res


# version 05
# speedup vs baseline: 1.0170x; 1.0170x over previous
import numpy as np
from contextlib import ExitStack

import concourse.bass as bass
import concourse.mybir as mybir
from concourse.bass_utils import run_bass_kernel_spmd

F32 = mybir.dt.float32
BF16 = mybir.dt.bfloat16
F16 = mybir.dt.float16
AF = mybir.ActivationFunctionType
ALU = mybir.AluOpType

NCORE = 8
T = 1024
HC = 1027
D = 2048
DIN = 4096
NH = 64
CH = 64
NCHUNK = 16
DIO = 8512
FFH = 5461
FFN_N = float(FFH * D)
SLICE_W = 11008


class Eng:
    def __init__(self, nc, ctx, name):
        self.sem = ctx.enter_context(nc.semaphore(name))
        self.n = 0

    def inc(self, instr, k=1):
        if k:
            instr.then_inc(self.sem, k)
            self.n += k
        return self.n

    def tok(self):
        return (self, self.n)


def _w(ectx, *toks):
    for t in toks:
        if t is not None:
            ectx.wait_ge(t[0].sem, t[1])


def build(tap=None):
    nc = bass.Bass("TRN2", target_bir_lowering=False)

    u_pad = nc.dram_tensor("u_pad", [HC, D], F32, kind="ExternalInput").ap()
    WinT = nc.dram_tensor("WinT", [D, DIO], F32, kind="ExternalInput").ap()
    WoutT = nc.dram_tensor("WoutT", [DIN, D], F32, kind="ExternalInput").ap()
    w1T = nc.dram_tensor("w1T", [D, FFH], F32, kind="ExternalInput").ap()
    w3T = nc.dram_tensor("w3T", [D, FFH], F32, kind="ExternalInput").ap()
    w2T = nc.dram_tensor("w2T", [FFH, D], F32, kind="ExternalInput").ap()
    wslice = nc.dram_tensor("wslice", [128, 3 * SLICE_W], F32, kind="ExternalInput").ap()
    conv_taps = nc.dram_tensor("conv_taps", [128, 136], F32, kind="ExternalInput").ap()
    conv_b_cm = nc.dram_tensor("conv_b_cm", [128, 34], F32, kind="ExternalInput").ap()
    D_cm = nc.dram_tensor("D_cm", [128, 32], F32, kind="ExternalInput").ap()
    normw_cm = nc.dram_tensor("normw_cm", [128, 32], F32, kind="ExternalInput").ap()
    dt_bias_c = nc.dram_tensor("dt_bias_c", [64, 1], F32, kind="ExternalInput").ap()
    A_log_c = nc.dram_tensor("A_log_c", [64, 1], F32, kind="ExternalInput").ap()
    idnf = nc.dram_tensor("idnf", [128, 128], F32, kind="ExternalInput").ap()
    idnb = nc.dram_tensor("idnb", [128, 128], BF16, kind="ExternalInput").ap()
    masknegd = nc.dram_tensor("masknegd", [64, 64], BF16, kind="ExternalInput").ap()
    wmaskT = nc.dram_tensor("wmaskT", [8, 8], F32, kind="ExternalInput").ap()
    qmask = nc.dram_tensor("qmask", [8, 1], F32, kind="ExternalInput").ap()
    out_d = nc.dram_tensor("out_d", [T, D], F32, kind="ExternalOutput").ap()

    dbg = None
    if tap is not None:
        dbg_shapes = {
            "zx": [128, 3072], "acm": [128, 4096], "conv": [128, 2048],
            "y": [128, 4096], "yall": [128, 32768], "ycorr": [128, 8192], "ynorm": [128, 2048], "x1": [128, 2048],
            "g": [128, 2048], "svals": [128, 16], "wout": [128, 2048], "wout2": [128, 2048], "corr2": [128, 2048], "p5": [128, 2048], "p5a": [128, 2048], "p5b": [128, 2048], "p5c": [128, 2048], "p5d": [128, 2048],
        }
        dbg = nc.dram_tensor("dbg", dbg_shapes[tap], F32, kind="ExternalOutput").ap()

    z_dram = nc.dram_tensor("z_dram", [128, 32 * T], F16)
    y_dram = nc.dram_tensor("y_dram", [128, NCHUNK * 2048], F16)
    ares_dram = nc.dram_tensor("ares_dram", [128, 16 * T], F32)
    x2_dram = nc.dram_tensor("x2_dram", [128, 16 * T], F32)
    acm_dram = nc.dram_tensor("acm_dram", [64, T], F32)
    aglob_dram = nc.dram_tensor("aglob_dram", [64, T], F32)
    alast_dram = nc.dram_tensor("alast_dram", [64, 16], F32)
    coef_dram = nc.dram_tensor("coef_dram", [8, 64], F32)
    E_loc = nc.dram_tensor("E_loc", [128, 4100], F32)
    E_all = nc.dram_tensor("E_all", [128 * NCORE, 4100], F32)
    sA_loc = nc.dram_tensor("sA_loc", [1, 4], F32)
    sA_all = nc.dram_tensor("sA_all", [1, 4], F32)

    with ExitStack() as CTX:
        E_ = CTX.enter_context
        sq = Eng(nc, CTX, "sq")
        gq = Eng(nc, CTX, "gq")
        pe = Eng(nc, CTX, "pe")
        dv = Eng(nc, CTX, "dv")
        ac = Eng(nc, CTX, "ac")
        po = Eng(nc, CTX, "po")
        cc = Eng(nc, CTX, "cc")
        ENGS = [sq, gq, pe, dv, ac, po]

        block = E_(nc.Block())

        def barrier():
            toks = [(e, e.n) for e in ENGS]
            for opener, eng in [(block.sync, sq), (block.gpsimd, gq), (block.tensor, pe),
                                (block.vector, dv), (block.scalar, ac)]:
                def f(ectx, eng=eng):
                    for (e, n) in toks:
                        if e is not eng and n > 0:
                            ectx.wait_ge(e.sem, n)
                opener(f)

        def finish(ectx, me):
            for e in ENGS:
                if e.n and e is not me:
                    ectx.wait_ge(e.sem, e.n)

        taps_s = E_(nc.sbuf_tensor("taps_s", [128, 136], F32))
        taps_bf = E_(nc.sbuf_tensor("taps_bf", [128, 136], BF16))
        convb_s = E_(nc.sbuf_tensor("convb_s", [128, 34], F32))
        Dcm_s = E_(nc.sbuf_tensor("Dcm_s", [128, 32], F32))
        nw_s = E_(nc.sbuf_tensor("nw_s", [128, 32], F32))
        dtb_s = E_(nc.sbuf_tensor("dtb_s", [64, 1], F32))
        alog_s = E_(nc.sbuf_tensor("alog_s", [64, 1], F32))
        idnf_s = E_(nc.sbuf_tensor("idnf_s", [128, 128], F32))
        idnb_s = E_(nc.sbuf_tensor("idnb_s", [128, 128], BF16))
        wmaskT_s = E_(nc.sbuf_tensor("wmaskT_s", [8, 8], F32))
        qmask_s = E_(nc.sbuf_tensor("qmask_s", [8, 1], F32))
        ones_col = E_(nc.sbuf_tensor("ones_col", [128, 1], F32))
        ones_col_b = E_(nc.sbuf_tensor("ones_col_b", [128, 1], BF16))
        ones_row = E_(nc.sbuf_tensor("ones_row", [1, 128], F32))
        ones65 = E_(nc.sbuf_tensor("ones65", [65, 128], F32))
        one64 = E_(nc.sbuf_tensor("one64", [64, 1], F32))
        eps1 = E_(nc.sbuf_tensor("eps1", [1, 1], F32))
        negA_s = E_(nc.sbuf_tensor("negA_s", [64, 1], F32))
        shalf_s = E_(nc.sbuf_tensor("shalf_s", [128, 4], F32))
        sneg_s = E_(nc.sbuf_tensor("sneg_s", [128, 4], F32))
        rstd1_b = E_(nc.sbuf_tensor("rstd1_b", [128, T], F32))
        am1_b = E_(nc.sbuf_tensor("am1_b", [128, T], F32))
        stat_row = E_(nc.sbuf_tensor("stat_row", [1, T], F32))
        stat_row2 = E_(nc.sbuf_tensor("stat_row2", [1, T], F32))
        ln_scr = E_(nc.sbuf_tensor("ln_scr", [1, T], F32))
        C_keep = E_(nc.sbuf_tensor("C_keep", [128, T], BF16))

        p_io0 = E_(nc.psum_tensor("p_io0", [128, 512], F32))
        p_io1 = E_(nc.psum_tensor("p_io1", [128, 512], F32))
        p_bc = E_(nc.psum_tensor("p_bc", [128, 512], F32))
        p_ms = E_(nc.psum_tensor("p_ms", [128, 512], F32))
        p_tb0 = E_(nc.psum_tensor("p_tb0", [128, 1024], BF16))
        p_tb1 = E_(nc.psum_tensor("p_tb1", [128, 1024], BF16))
        p_y0 = E_(nc.psum_tensor("p_y0", [128, 512], F32))
        p_y1 = E_(nc.psum_tensor("p_y1", [128, 512], F32))

        @block.sync
        def _(sync):
            for dst, src in [(taps_s, conv_taps), (convb_s, conv_b_cm), (Dcm_s, D_cm),
                             (nw_s, normw_cm), (dtb_s, dt_bias_c), (alog_s, A_log_c),
                             (idnf_s, idnf), (idnb_s, idnb),
                             (wmaskT_s, wmaskT), (qmask_s, qmask)]:
                sq.inc(sync.dma_start(out=dst[:, :], in_=src), 16)
        t_cl = sq.tok()

        @block.vector
        def _(vector):
            _w(vector, t_cl)
            vector.memset(ones_col[:, :], 1.0)
            vector.memset(ones_col_b[:, :], 1.0)
            vector.memset(ones_row[:, :], 1.0)
            vector.memset(ones65[:, :], 1.0)
            vector.memset(one64[:, :], 1.0)
            dv.inc(vector.memset(eps1[:, :], 1e-5))
        t_ms0 = dv.tok()

        @block.scalar
        def _(scalar):
            _w(scalar, t_ms0)
            ac.inc(scalar.activation(out=negA_s[:, :], in_=alog_s[:, :], func=AF.Exp))
            ac.inc(scalar.activation(out=taps_bf[:, :], in_=taps_s[:, :], func=AF.Copy))
        t_eA = ac.tok()

        @block.vector
        def _(vector):
            _w(vector, t_eA)
            dv.inc(vector.tensor_scalar(out=negA_s[:, :], in0=negA_s[:, :],
                                        scalar1=-1.0, scalar2=None, op0=ALU.mult))
        t_negA = dv.tok()

        with ExitStack() as RX:
            EX = RX.enter_context
            xbc_s = EX(nc.sbuf_tensor("xbc_s", [128, 34 * HC], BF16))
            dtv_s = EX(nc.sbuf_tensor("dtv_s", [64, T], F32))
            Acm_s = EX(nc.sbuf_tensor("Acm_s", [64, T], F32))
            acm_s = EX(nc.sbuf_tensor("acm_s", [64, T], F32))
            aglob_s = EX(nc.sbuf_tensor("aglob_s", [64, T], F32))
            alast_s = EX(nc.sbuf_tensor("alast_s", [64, 64], F32))
            O_s = EX(nc.sbuf_tensor("O_s", [64, 16], F32))

            with ExitStack() as R1:
                E1 = R1.enter_context
                u_tm = [E1(nc.sbuf_tensor(f"u_tm{i}", [128, 2048], F32)) for i in range(2)]
                u_cmb = E1(nc.sbuf_tensor("u_cmb", [128, 16 * HC], BF16))
                wf = [E1(nc.sbuf_tensor(f"wf{i}", [128, 16 * 128], F32)) for i in range(2)]
                wb = [E1(nc.sbuf_tensor(f"wb{i}", [128, 16 * 128], BF16)) for i in range(2)]
                z_st = [E1(nc.sbuf_tensor(f"z_st{i}", [128, 512], F16)) for i in range(2)]
                dtraw_s = E1(nc.sbuf_tensor("dtraw_s", [64, T], F32))
                scr64 = E1(nc.sbuf_tensor("scr64", [64, T], F32))

                udma_t = [None, None]
                uev_t = [None, None]
                bank_free = [None, None]
                for i in range(9):
                    rows = 128 if i < 8 else 3
                    par = i % 2

                    @block.sync
                    def _(sync, i=i, rows=rows, par=par):
                        _w(sync, uev_t[par])
                        sq.inc(sync.dma_start(out=u_tm[par][0:rows, :],
                                              in_=u_pad[i * 128:i * 128 + rows, :]), 16)
                    udma_t[par] = sq.tok()

                    for grp in range(4):
                        bank = [p_io0, p_io1][grp % 2]

                        @block.tensor
                        def _(tensor, rows=rows, par=par, grp=grp, bank=bank):
                            _w(tensor, udma_t[par], bank_free[grp % 2])
                            for j in range(4):
                                g = grp * 4 + j
                                pe.inc(tensor.transpose(bank[0:128, j * 128:j * 128 + rows],
                                                        u_tm[par][0:rows, g * 128:(g + 1) * 128],
                                                        idnf_s[0:rows, 0:rows]))
                        t_tp = pe.tok()

                        @block.vector
                        def _(vector, i=i, rows=rows, grp=grp, bank=bank, t_tp=t_tp):
                            _w(vector, t_tp)
                            for j in range(4):
                                g = grp * 4 + j
                                dv.inc(vector.tensor_copy(
                                    out=u_cmb[:, g * HC + i * 128: g * HC + i * 128 + rows],
                                    in_=bank[0:128, j * 128:j * 128 + rows]))
                        bank_free[grp % 2] = dv.tok()
                    uev_t[par] = dv.tok()
                u_ready = dv.tok()

                zxdt_tiles = [(3, 512), (515, 512)]
                xbc_tiles = [(0, 343), (343, 342), (685, 342)]
                blocks = [(m, 1) for m in range(67)]

                wdma_t = [None, None]
                wcast_t = [None, None]
                wused_t = [None, None]
                io_free = [bank_free[0], bank_free[1]]
                zst_free = [None, None]
                zct = 0
                for bi, (m0, mcnt) in enumerate(blocks):
                    par = bi % 2
                    mw = 128 * mcnt if m0 != 66 else 64

                    @block.sync
                    def _(sync, m0=m0, mw=mw, par=par):
                        _w(sync, wcast_t[par])
                        src = WinT.rearrange("(kc kp) m -> kp kc m", kp=128)[:, :, m0 * 128:m0 * 128 + mw]
                        dst = wf[par][:, 0:16 * mw].rearrange("p (kc m) -> p kc m", kc=16)
                        sq.inc(sync.dma_start(out=dst, in_=src), 16)
                    wdma_t[par] = sq.tok()

                    @block.gpsimd
                    def _(gpsimd, mw=mw, par=par):
                        _w(gpsimd, wdma_t[par], wused_t[par])
                        po.inc(gpsimd.tensor_copy(out=wb[par][:, 0:16 * mw], in_=wf[par][:, 0:16 * mw]))
                    wcast_t[par] = po.tok()

                    for mi in range(mcnt):
                        m = m0 + mi
                        rows = 64 if m == 66 else 128
                        tiles = xbc_tiles if 32 <= m < 66 else zxdt_tiles
                        for ti, (t0, tn) in enumerate(tiles):
                            bpar = zct % 2
                            zct += 1
                            bank = [p_io0, p_io1][bpar]

                            @block.tensor
                            def _(tensor, bi=bi, m=m, mi=mi, mw=mw, par=par, rows=rows,
                                  t0=t0, tn=tn, bank=bank, bpar=bpar):
                                _w(tensor, wcast_t[par], io_free[bpar])
                                if bi == 0:
                                    _w(tensor, u_ready)
                                moff = mi * 128
                                for k in range(16):
                                    lhsT = wb[par][:, k * mw + moff: k * mw + moff + rows]
                                    rhs = u_cmb[:, k * HC + t0: k * HC + t0 + tn]
                                    pe.inc(tensor.matmul(bank[0:rows, 0:tn], lhsT, rhs,
                                                         start=(k == 0), stop=(k == 15)),
                                           1 if k == 15 else 0)
                            t_mm = pe.tok()
                            wused_t[par] = t_mm

                            if m < 32:
                                zp = zct % 2

                                @block.scalar
                                def _(scalar, bank=bank, t_mm=t_mm, tn=tn, zp=zp):
                                    _w(scalar, t_mm, zst_free[zp])
                                    ac.inc(scalar.activation(out=z_st[zp][:, 0:tn], in_=bank[:, 0:tn],
                                                             func=AF.Silu))
                                t_z = ac.tok()
                                io_free[bpar] = t_z

                                @block.gpsimd
                                def _(gpsimd, m=m, t0=t0, tn=tn, zp=zp, t_z=t_z):
                                    _w(gpsimd, t_z)
                                    gq.inc(gpsimd.dma_start(
                                        out=z_dram.ap()[:, m * T + (t0 - 3): m * T + (t0 - 3) + tn],
                                        in_=z_st[zp][:, 0:tn]), 16)
                                zst_free[zp] = gq.tok()
                            elif m < 66:
                                @block.scalar
                                def _(scalar, bank=bank, t_mm=t_mm, m=m, t0=t0, tn=tn):
                                    _w(scalar, t_mm)
                                    g = m - 32
                                    ac.inc(scalar.activation(out=xbc_s[:, g * HC + t0: g * HC + t0 + tn],
                                                             in_=bank[:, 0:tn], func=AF.Copy))
                                io_free[bpar] = ac.tok()
                            else:
                                @block.scalar
                                def _(scalar, bank=bank, t_mm=t_mm, t0=t0, tn=tn):
                                    _w(scalar, t_mm)
                                    ac.inc(scalar.activation(out=dtraw_s[:, t0 - 3: t0 - 3 + tn],
                                                             in_=bank[0:64, 0:tn], func=AF.Copy))
                                io_free[bpar] = ac.tok()
                z_gq_done = gq.tok()
                p2_ac = ac.tok()

                if tap == "zx":
                    @block.gpsimd
                    def _(gpsimd):
                        finish(gpsimd, gq)
                        gq.inc(gpsimd.dma_start(out=dbg[:, 0:1024], in_=z_dram.ap()[:, 0:1024]), 16)
                        gq.inc(gpsimd.dma_start(out=dbg[:, 1024:2048], in_=xbc_s[:, 0:1024]), 16)
                        gq.inc(gpsimd.dma_start(out=dbg[0:64, 2048:3072], in_=dtraw_s[:, :]), 16)
                        gpsimd.wait_ge(gq.sem, gq.n)
                    return nc

                @block.vector
                def _(vector):
                    _w(vector, p2_ac, t_negA)
                    dv.inc(vector.tensor_scalar(out=dtraw_s[:, :], in0=dtraw_s[:, :],
                                                scalar1=dtb_s[:, :], scalar2=None, op0=ALU.add))
                    dv.inc(vector.scalar_tensor_tensor(out=scr64[:, :], in0=dtraw_s[:, :],
                                                       scalar=-1.0, in1=dtraw_s[:, :],
                                                       op0=ALU.mult, op1=ALU.max))
                t_sp1 = dv.tok()

                @block.scalar
                def _(scalar):
                    _w(scalar, t_sp1)
                    ac.inc(scalar.activation(out=scr64[:, :], in_=scr64[:, :], func=AF.Exp, scale=-1.0))
                    ac.inc(scalar.activation(out=scr64[:, :], in_=scr64[:, :], func=AF.Ln, bias=one64[:, :]))
                t_sp2 = ac.tok()

                @block.vector
                def _(vector):
                    _w(vector, t_sp2)
                    dv.inc(vector.tensor_scalar(out=dtraw_s[:, :], in0=dtraw_s[:, :],
                                                scalar1=0.0, scalar2=None, op0=ALU.max))
                    dv.inc(vector.tensor_tensor(out=dtv_s[:, :], in0=dtraw_s[:, :], in1=scr64[:, :],
                                                op=ALU.add))
                    dv.inc(vector.tensor_scalar(out=Acm_s[:, :], in0=dtv_s[:, :],
                                                scalar1=negA_s[:, :], scalar2=None, op0=ALU.mult))
                    vector.wait_ge(dv.sem, dv.n)
                    for c in range(NCHUNK):
                        dv.inc(vector.tensor_tensor_scan(out=acm_s[:, c * CH:(c + 1) * CH],
                                                         data0=one64[:, :].broadcast_to([64, CH]),
                                                         data1=Acm_s[:, c * CH:(c + 1) * CH],
                                                         initial=0.0, op0=ALU.mult, op1=ALU.add))
                    vector.memset(alast_s[:, :], 0.0)
                    dv.inc(vector.tensor_copy(out=alast_s[:, 0:16], in_=acm_s[:, CH - 1::CH]))
                    vector.wait_ge(dv.sem, dv.n)
                    dv.inc(vector.tensor_tensor_scan(out=scr64[:, 0:64],
                                                     data0=one64[:, :].broadcast_to([64, 64]),
                                                     data1=alast_s[:, 0:64], initial=0.0,
                                                     op0=ALU.mult, op1=ALU.add))
                    vector.wait_ge(dv.sem, dv.n)
                    vector.memset(O_s[:, 0:1], 0.0)
                    dv.inc(vector.tensor_copy(out=O_s[:, 1:16], in_=scr64[:, 0:15]))
                    vector.wait_ge(dv.sem, dv.n)
                    dv.inc(vector.tensor_tensor(
                        out=aglob_s.ap().rearrange("p (c l) -> p c l", c=16),
                        in0=acm_s.ap().rearrange("p (c l) -> p c l", c=16),
                        in1=O_s[:, :, None].broadcast_to([64, 16, CH]), op=ALU.add))
                t_acm = dv.tok()

                @block.sync
                def _(sync):
                    _w(sync, t_acm)
                    sq.inc(sync.dma_start(out=acm_dram.ap(), in_=acm_s[:, :]), 16)
                    sq.inc(sync.dma_start(out=aglob_dram.ap(), in_=aglob_s[:, :]), 16)
                    sq.inc(sync.dma_start(out=alast_dram.ap(), in_=alast_s[:, 0:16]), 16)
                    sq.inc(sync.dma_start(out=E_loc.ap()[0:64, 4096:4097], in_=scr64[:, 15:16], allow_slow_non_contiguous=True), 16)
                t_acmdma = sq.tok()

                if tap == "acm":
                    @block.gpsimd
                    def _(gpsimd):
                        finish(gpsimd, gq)
                        gq.inc(gpsimd.dma_start(out=dbg[0:64, 0:1024], in_=dtv_s[:, :]), 16)
                        gq.inc(gpsimd.dma_start(out=dbg[0:64, 1024:2048], in_=Acm_s[:, :]), 16)
                        gq.inc(gpsimd.dma_start(out=dbg[0:64, 2048:3072], in_=acm_s[:, :]), 16)
                        gq.inc(gpsimd.dma_start(out=dbg[0:64, 3072:4096], in_=aglob_s[:, :]), 16)
                        gq.inc(gpsimd.dma_start(out=dbg[64:128, 0:64], in_=alast_s[:, :]), 16)
                        gq.inc(gpsimd.dma_start(out=dbg[64:128, 64:128], in_=scr64[:, 0:64]), 16)
                        gq.inc(gpsimd.dma_start(out=dbg[64:128, 128:144], in_=O_s[:, :]), 16)
                        gpsimd.wait_ge(gq.sem, gq.n)
                    return nc

                barrier()

            with ExitStack() as R2:
                E2 = R2.enter_context
                ct = [E2(nc.sbuf_tensor(f"ct{i}", [128, T], F32)) for i in range(2)]
                wsl = E2(nc.sbuf_tensor("wsl", [128, SLICE_W], F32))
                red86 = E2(nc.sbuf_tensor("red86", [128, 86], F32))
                red1 = E2(nc.sbuf_tensor("red1", [128, 4], F32))
                sA_s = E2(nc.sbuf_tensor("sA_s", [1, 4], F32))
                svals = E2(nc.sbuf_tensor("svals", [1, 4], F32))

                conv_dve_t = [None] * 34
                conv_act_t = [None] * 34
                for g in range(34):
                    par = g % 2

                    @block.vector
                    def _(vector, g=g, par=par):
                        if g >= 2:
                            _w(vector, conv_act_t[g - 2])
                        gb = g * HC
                        vector.tensor_tensor(out=ct[par][:, :], in0=xbc_s[:, gb: gb + T],
                                             in1=taps_bf[:, g:g + 1].broadcast_to([128, T]), op=ALU.mult)
                        ins = None
                        for k in (1, 2, 3):
                            ins = vector.scalar_tensor_tensor(out=ct[par][:, :],
                                                              in0=xbc_s[:, gb + k: gb + k + T],
                                                              scalar=taps_bf[:, k * 34 + g: k * 34 + g + 1],
                                                              in1=ct[par][:, :], op0=ALU.mult, op1=ALU.add)
                        dv.inc(ins)
                    conv_dve_t[g] = dv.tok()

                    @block.scalar
                    def _(scalar, g=g, par=par):
                        _w(scalar, conv_dve_t[g])
                        ac.inc(scalar.activation(out=xbc_s[:, g * HC: g * HC + T], in_=ct[par][:, :],
                                                 func=AF.Silu, bias=convb_s[:, g:g + 1]))
                    conv_act_t[g] = ac.tok()
                conv_done = ac.tok()

                wsl_free = None
                for wi in range(3):
                    @block.sync
                    def _(sync, wi=wi):
                        _w(sync, wsl_free)
                        sq.inc(sync.dma_start(out=wsl[:, :],
                                              in_=wslice[:, wi * SLICE_W:(wi + 1) * SLICE_W]), 16)
                    t_wd = sq.tok()

                    @block.vector
                    def _(vector, wi=wi, t_wd=t_wd):
                        _w(vector, t_wd)
                        dv.inc(vector.tensor_reduce(out=red86[:, :],
                                                    in_=wsl.ap().rearrange("p (a b) -> p a b", a=86),
                                                    axis=mybir.AxisListType.X, op=ALU.add,
                                                    apply_absolute_value=True))
                        dv.inc(vector.tensor_reduce(out=red1[:, wi:wi + 1], in_=red86[:, :],
                                                    axis=mybir.AxisListType.X, op=ALU.add))
                    wsl_free = dv.tok()
                t_redA = dv.tok()

                @block.tensor
                def _(tensor):
                    _w(tensor, t_redA)
                    for wi in range(3):
                        pe.inc(tensor.matmul(p_ms[0:1, wi:wi + 1], ones_col[:, :], red1[:, wi:wi + 1],
                                             start=True, stop=True))
                t_sApe = pe.tok()

                @block.scalar
                def _(scalar):
                    _w(scalar, t_sApe)
                    ac.inc(scalar.activation(out=sA_s[0:1, 0:4], in_=p_ms[0:1, 0:4], func=AF.Copy))
                t_sA = ac.tok()

                @block.gpsimd
                def _(gpsimd):
                    _w(gpsimd, t_sA)
                    gq.inc(gpsimd.dma_start(out=sA_loc.ap(), in_=sA_s[0:1, :]), 16)
                    gpsimd.wait_ge(gq.sem, gq.n)
                    cc.inc(gpsimd.collective_compute(
                        "AllReduce", ALU.add, replica_groups=[list(range(NCORE))],
                        ins=[sA_loc.ap().opt()], outs=[sA_all.ap().opt()]))
                    gpsimd.wait_ge(cc.sem, cc.n)
                    gq.inc(gpsimd.dma_start(out=sA_s[0:1, :], in_=sA_all.ap()), 16)
                t_sAr = gq.tok()

                @block.vector
                def _(vector):
                    _w(vector, t_sAr)
                    dv.inc(vector.tensor_scalar(out=svals[:, :], in0=sA_s[0:1, :],
                                                scalar1=1.0 / FFN_N, scalar2=None, op0=ALU.mult))
                t_sv = dv.tok()

                @block.tensor
                def _(tensor):
                    _w(tensor, t_sv, t_sApe)
                    pe.inc(tensor.matmul(p_ms[0:128, 4:8], ones_row[:, :], svals[0:1, :],
                                         start=True, stop=True))
                t_svb = pe.tok()

                @block.vector
                def _(vector):
                    _w(vector, t_svb)
                    dv.inc(vector.tensor_scalar(out=shalf_s[:, :], in0=p_ms[:, 4:8],
                                                scalar1=0.5, scalar2=None, op0=ALU.mult))
                    dv.inc(vector.tensor_scalar(out=sneg_s[:, :], in0=p_ms[:, 4:8],
                                                scalar1=-0.5, scalar2=None, op0=ALU.mult))
                t_shalf = dv.tok()

                if tap == "conv":
                    @block.gpsimd
                    def _(gpsimd):
                        finish(gpsimd, gq)
                        gq.inc(gpsimd.dma_start(out=dbg[:, 0:1024], in_=xbc_s[:, 0:1024]), 16)
                        gq.inc(gpsimd.dma_start(out=dbg[:, 1024:2048], in_=xbc_s[:, 33 * HC:33 * HC + 1024]), 16)
                        gpsimd.wait_ge(gq.sem, gq.n)
                    return nc
                if tap == "svals":
                    @block.gpsimd
                    def _(gpsimd):
                        finish(gpsimd, gq)
                        gq.inc(gpsimd.dma_start(out=dbg[:, 0:4], in_=shalf_s[:, :]), 16)
                        gq.inc(gpsimd.dma_start(out=dbg[0:1, 4:8], in_=sA_s[0:1, :]), 16)
                        gpsimd.wait_ge(gq.sem, gq.n)
                    return nc
                barrier()

            with ExitStack() as R3:
                E3 = R3.enter_context
                mneg_s = E3(nc.sbuf_tensor("mneg_s", [64, 64], BF16))
                dbc_s = E3(nc.sbuf_tensor("dbc_s", [128, 1024], F32))
                aflat = E3(nc.sbuf_tensor("aflat", [1, 4096], F32))
                Ltmp = E3(nc.sbuf_tensor("Ltmp", [64, 4096], F32))
                M_bf = E3(nc.sbuf_tensor("M_bf", [64, 4096], F16))
                Csc = E3(nc.sbuf_tensor("Csc", [128, 4096], F16))
                x2_s = E3(nc.sbuf_tensor("x2_s", [64, 4096], F16))
                x2d_s = E3(nc.sbuf_tensor("x2d_s", [64, 4096], F16))
                at_s = E3(nc.sbuf_tensor("at_s", [64, 64], F32))
                dtt_s = E3(nc.sbuf_tensor("dtt_s", [64, 64], F32))
                Wcm_s = E3(nc.sbuf_tensor("Wcm_s", [64, 64], F32))
                Wt_s = E3(nc.sbuf_tensor("Wt_s", [64, 64], F32))
                Bt_s = E3(nc.sbuf_tensor("Bt_s", [64, 128], F16))
                S_run = E3(nc.sbuf_tensor("S_run", [128, 4096], F32))
                S_in_bf = E3(nc.sbuf_tensor("S_in_bf", [128, 4096], F16))
                y_st = [E3(nc.sbuf_tensor("y_st0", [128, 2048], F16))] * 2

                @block.sync
                def _(sync):
                    sq.inc(sync.dma_start(out=mneg_s[:, :], in_=masknegd), 16)
                t_mneg = sq.tok()

                @block.gpsimd
                def _(gpsimd):
                    _w(gpsimd, t_acmdma)
                    gq.inc(gpsimd.dma_start(out=aflat[0:1, 0:1024],
                                            in_=alast_dram.ap().rearrange("h c -> (h c)").unsqueeze(0)), 16)
                t_af = gq.tok()

                @block.tensor
                def _(tensor):
                    _w(tensor, t_af, io_free[0], io_free[1])
                    pe.inc(tensor.matmul(p_io0[:, 0:512], ones_row[:, :], aflat[0:1, 0:512],
                                         start=True, stop=True))
                    pe.inc(tensor.matmul(p_io1[:, 0:512], ones_row[:, :], aflat[0:1, 512:1024],
                                         start=True, stop=True))
                t_dbc = pe.tok()

                @block.scalar
                def _(scalar):
                    _w(scalar, t_dbc)
                    ac.inc(scalar.activation(out=dbc_s[:, 0:512], in_=p_io0[:, 0:512], func=AF.Exp))
                    ac.inc(scalar.activation(out=dbc_s[:, 512:1024], in_=p_io1[:, 0:512], func=AF.Exp))
                t_dbcs = ac.tok()

                @block.vector
                def _(vector):
                    _w(vector, t_dbcs)
                    dv.inc(vector.memset(S_run[:, :], 0.0))
                scan_t = dv.tok()

                aflat_t = [None, None]
                apex_t = [None, None]
                ydma_t = [None, None]
                cast_t = None
                yB_free = [None, None]
                sB_free = [t_dbcs, t_dbcs]
                last_ymm = None
                last_Mbf = None
                last_csc = None
                last_x2d = None
                last_smm = None
                last_w4 = None
                xev_tok = [None, None]

                for c in range(NCHUNK):
                    par = c % 2
                    arow = 0

                    @block.gpsimd
                    def _(gpsimd, c=c):
                        _w(gpsimd, t_acmdma, apex_t[0])
                        src = acm_dram.ap()[None, :, c * CH:(c + 1) * CH]
                        dst = aflat[0:1, :].rearrange("p (h l) -> p h l", h=64)
                        gq.inc(gpsimd.dma_start(out=dst, in_=src), 16)
                    aflat_t[0] = gq.tok()
                    aflat_t[1] = aflat_t[0]

                    for j in range(8):
                        ybk = j % 2
                        bank = [p_y0, p_y1][ybk]

                        @block.tensor
                        def _(tensor, par=par, j=j, bank=bank, ybk=ybk, arow=arow):
                            _w(tensor, aflat_t[par], yB_free[ybk])
                            pe.inc(tensor.matmul(bank[0:64, 0:512], ones_row[0:1, 0:64],
                                                 aflat[0:1, j * 512:(j + 1) * 512],
                                                 start=True, stop=True))
                        t_R = pe.tok()

                        @block.vector
                        def _(vector, j=j, bank=bank, t_R=t_R):
                            _w(vector, t_R, t_mneg)
                            dv.inc(vector.tensor_tensor(
                                out=Ltmp[:, j * 512:(j + 1) * 512].rearrange("p (h l) -> p h l", h=8),
                                in0=bank[0:64, 0:512].rearrange("p (h l) -> p h l", h=8),
                                in1=mneg_s[:, None, :].broadcast_to([64, 8, 64]), op=ALU.add))
                        yB_free[ybk] = dv.tok()

                    @block.tensor
                    def _(tensor, c=c):
                        _w(tensor, t_acm, last_Mbf)
                        pe.inc(tensor.transpose(p_bc[0:64, 0:64], acm_s[:, c * CH:(c + 1) * CH],
                                                idnf_s[0:64, 0:64]))
                        pe.inc(tensor.transpose(p_bc[0:64, 64:128], dtv_s[:, c * CH:(c + 1) * CH],
                                                idnf_s[0:64, 0:64]))
                    t_tpa = pe.tok()

                    @block.vector
                    def _(vector, t_tpa=t_tpa):
                        _w(vector, t_tpa)
                        dv.inc(vector.tensor_copy(out=at_s[:, :], in_=p_bc[0:64, 0:64]))
                        dv.inc(vector.tensor_copy(out=dtt_s[:, :], in_=p_bc[0:64, 64:128]))
                    t_atd = dv.tok()

                    @block.vector
                    def _(vector, c=c, lw4=last_w4):
                        _w(vector, lw4)
                        dv.inc(vector.tensor_scalar(out=Wcm_s[:, :], in0=acm_s[:, c * CH:(c + 1) * CH],
                                                    scalar1=alast_s[:, c:c + 1], scalar2=None,
                                                    op0=ALU.subtract))
                    t_w1 = dv.tok()

                    @block.scalar
                    def _(scalar, t_w1=t_w1):
                        _w(scalar, t_w1)
                        ac.inc(scalar.activation(out=Wcm_s[:, :], in_=Wcm_s[:, :], func=AF.Exp, scale=-1.0))
                    t_w2 = ac.tok()

                    @block.vector
                    def _(vector, c=c, t_w2=t_w2):
                        _w(vector, t_w2)
                        dv.inc(vector.tensor_tensor(out=Wcm_s[:, :], in0=Wcm_s[:, :],
                                                    in1=dtv_s[:, c * CH:(c + 1) * CH], op=ALU.mult))
                    t_w3 = dv.tok()

                    @block.tensor
                    def _(tensor, t_w3=t_w3):
                        _w(tensor, t_w3)
                        pe.inc(tensor.transpose(p_ms[0:64, 0:64], Wcm_s[:, :], idnf_s[0:64, 0:64]))
                    t_w4 = pe.tok()
                    last_w4 = t_w4

                    @block.vector
                    def _(vector, t_w4=t_w4):
                        _w(vector, t_w4)
                        dv.inc(vector.tensor_copy(out=Wt_s[:, :], in_=p_ms[0:64, 0:64]))
                    t_wt = dv.tok()

                    @block.tensor
                    def _(tensor, c=c, t_wt=t_wt):
                        _w(tensor, conv_done, t_wt)
                        pe.inc(tensor.matmul(p_ms[0:64, 64:128],
                                             xbc_s[:, 32 * HC + c * CH: 32 * HC + c * CH + CH],
                                             xbc_s[:, 33 * HC + c * CH: 33 * HC + c * CH + CH],
                                             start=True, stop=True))
                    t_G = pe.tok()

                    @block.vector
                    def _(vector, t_atd=t_atd):
                        _w(vector, t_atd)
                        dv.inc(vector.tensor_tensor(
                            out=Ltmp.ap().rearrange("p (h l) -> p h l", h=64),
                            in0=Ltmp.ap().rearrange("p (h l) -> p h l", h=64),
                            in1=at_s[:, :, None].broadcast_to([64, 64, 64]), op=ALU.subtract))
                    t_L1 = dv.tok()

                    @block.scalar
                    def _(scalar, t_L1=t_L1):
                        _w(scalar, t_L1)
                        ac.inc(scalar.activation(out=Ltmp[:, :], in_=Ltmp[:, :], func=AF.Exp))
                    t_L2 = ac.tok()

                    @block.vector
                    def _(vector, t_L2=t_L2, t_G=t_G, lym=last_ymm):
                        _w(vector, t_L2, t_G, lym)
                        dv.inc(vector.tensor_tensor(
                            out=Ltmp.ap().rearrange("p (h l) -> p h l", h=64),
                            in0=Ltmp.ap().rearrange("p (h l) -> p h l", h=64),
                            in1=dtt_s[:, :, None].broadcast_to([64, 64, 64]), op=ALU.mult))
                        dv.inc(vector.tensor_tensor(
                            out=M_bf.ap().rearrange("p (h l) -> p h l", h=64),
                            in0=Ltmp.ap().rearrange("p (h l) -> p h l", h=64),
                            in1=p_ms[0:64, None, 64:128].broadcast_to([64, 64, 64]), op=ALU.mult))
                    last_Mbf = dv.tok()

                    for grp in range(4):
                        tbk = grp % 2
                        tb = [p_tb0, p_tb1][tbk]

                        @block.tensor
                        def _(tensor, c=c, grp=grp, tb=tb, tbk=tbk, lx2d=last_x2d):
                            _w(tensor, conv_done, xev_tok[tbk])
                            if grp == 0:
                                _w(tensor, lx2d)
                            for j in range(8):
                                g = grp * 8 + j
                                pe.inc(tensor.transpose(tb[0:64, j * 128:(j + 1) * 128],
                                                        xbc_s[:, g * HC + c * CH: g * HC + c * CH + CH],
                                                        idnb_s[:, :]))
                        t_xtp = pe.tok()

                        @block.scalar
                        def _(scalar, grp=grp, tb=tb, t_xtp=t_xtp):
                            _w(scalar, t_xtp)
                            for j in range(8):
                                g = grp * 8 + j
                                ac.inc(scalar.activation(out=x2_s[:, g * 128:(g + 1) * 128],
                                                         in_=tb[0:64, j * 128:(j + 1) * 128], func=AF.Copy))
                        xev_tok[tbk] = ac.tok()
                    t_x2 = ac.tok()

                    @block.vector
                    def _(vector, t_x2=t_x2, t_wt=t_wt, lsmm=last_smm):
                        _w(vector, t_x2, t_wt, lsmm)
                        dv.inc(vector.tensor_tensor(
                            out=x2d_s.ap().rearrange("p (h q) -> p h q", h=64),
                            in0=x2_s.ap().rearrange("p (h q) -> p h q", h=64),
                            in1=Wt_s[:, :, None].broadcast_to([64, 64, 64]), op=ALU.mult))
                    last_x2d = dv.tok()

                    @block.tensor
                    def _(tensor, c=c):
                        _w(tensor, conv_done, xev_tok[0])
                        pe.inc(tensor.transpose(p_tb0[0:64, 0:128],
                                                xbc_s[:, 32 * HC + c * CH: 32 * HC + c * CH + CH],
                                                idnb_s[:, :]))
                    t_Btp = pe.tok()

                    @block.scalar
                    def _(scalar, t_Btp=t_Btp):
                        _w(scalar, t_Btp)
                        ac.inc(scalar.activation(out=Bt_s[:, :], in_=p_tb0[0:64, 0:128], func=AF.Copy))
                    t_Bt = ac.tok()
                    xev_tok[0] = t_Bt

                    for j in range(8):
                        ybk = j % 2
                        bank = [p_y0, p_y1][ybk]

                        @block.tensor
                        def _(tensor, par=par, j=j, bank=bank, ybk=ybk, arow=arow):
                            _w(tensor, aflat_t[par], yB_free[ybk])
                            pe.inc(tensor.matmul(bank[0:128, 0:512], ones_row[:, :],
                                                 aflat[0:1, j * 512:(j + 1) * 512],
                                                 start=True, stop=True))
                        t_e = pe.tok()

                        @block.scalar
                        def _(scalar, j=j, bank=bank, t_e=t_e, lcsc=last_csc):
                            _w(scalar, t_e, lcsc)
                            ac.inc(scalar.activation(out=Csc[:, j * 512:(j + 1) * 512],
                                                     in_=bank[0:128, 0:512], func=AF.Exp))
                        yB_free[ybk] = ac.tok()
                    apex_t[0] = pe.tok()
                    t_expa = ac.tok()

                    @block.vector
                    def _(vector, c=c, t_expa=t_expa, lym=last_ymm):
                        _w(vector, t_expa, lym)
                        dv.inc(vector.tensor_tensor(
                            out=Csc.ap().rearrange("p (h l) -> p h l", h=64),
                            in0=Csc.ap().rearrange("p (h l) -> p h l", h=64),
                            in1=xbc_s[:, None, 33 * HC + c * CH: 33 * HC + c * CH + CH].broadcast_to([128, 64, 64]),
                            op=ALU.mult))
                    last_csc = dv.tok()

                    @block.gpsimd
                    def _(gpsimd, scan_t=scan_t, lym=last_ymm):
                        _w(gpsimd, scan_t, lym)
                        po.inc(gpsimd.tensor_copy(out=S_in_bf[:, :], in_=S_run[:, :]))
                    cast_t = po.tok()

                    for half in range(2):
                        for sub in range(2):
                            bank = [p_y0, p_y1][sub]

                            @block.tensor
                            def _(tensor, half=half, sub=sub, bank=bank,
                                  lM=last_Mbf, lC=last_csc, ct_=cast_t, tx2=t_x2):
                                _w(tensor, lM, lC, ct_, tx2, yB_free[sub])
                                for j in range(8):
                                    hp = half * 16 + sub * 8 + j
                                    for hh in range(2):
                                        h = 2 * hp + hh
                                        o = bank[hh * 64:hh * 64 + 64, j * 64:(j + 1) * 64]
                                        tensor.matmul(o, x2_s[:, hp * 128 + hh * 64: hp * 128 + hh * 64 + 64],
                                                      M_bf[:, h * 64:(h + 1) * 64], start=True, stop=False)
                                        pe.inc(tensor.matmul(o, S_in_bf[:, h * 64:(h + 1) * 64],
                                                             Csc[:, h * 64:(h + 1) * 64],
                                                             start=False, stop=True))
                            t_ymm = pe.tok()

                            @block.vector
                            def _(vector, c=c, half=half, sub=sub, bank=bank, t_ymm=t_ymm, par=par):
                                _w(vector, t_ymm)
                                if half == 0 and sub == 0:
                                    _w(vector, ydma_t[0])
                                for j in range(8):
                                    hp = half * 16 + sub * 8 + j
                                    dv.inc(vector.scalar_tensor_tensor(
                                        out=y_st[par][:, hp * 64:(hp + 1) * 64],
                                        in0=xbc_s[:, hp * HC + c * CH: hp * HC + c * CH + CH],
                                        scalar=Dcm_s[:, hp:hp + 1],
                                        in1=bank[:, j * 64:(j + 1) * 64],
                                        op0=ALU.mult, op1=ALU.add))
                            yB_free[sub] = dv.tok()
                    last_ymm = pe.tok()
                    t_yev = dv.tok()

                    @block.gpsimd
                    def _(gpsimd, c=c, par=par, t_yev=t_yev):
                        _w(gpsimd, t_yev)
                        gq.inc(gpsimd.dma_start(out=y_dram.ap()[:, c * 2048:(c + 1) * 2048],
                                                in_=y_st[par][:, :]), 16)
                    ydma_t[0] = gq.tok()
                    ydma_t[1] = gq.tok()

                    @block.vector
                    def _(vector, c=c, ct_=cast_t):
                        _w(vector, ct_)
                        dv.inc(vector.tensor_tensor(
                            out=S_run.ap().rearrange("p (h q) -> p h q", h=64),
                            in0=S_run.ap().rearrange("p (h q) -> p h q", h=64),
                            in1=dbc_s[:, c::16][:, :, None].broadcast_to([128, 64, 64]),
                            op=ALU.mult))
                    t_decS = dv.tok()

                    for j in range(8):
                        sbk = j % 2
                        bank = [p_io0, p_io1][sbk]

                        @block.tensor
                        def _(tensor, j=j, bank=bank, sbk=sbk, t_Bt=t_Bt, lx2d=last_x2d):
                            _w(tensor, t_Bt, lx2d, sB_free[sbk])
                            pe.inc(tensor.matmul(bank[0:128, 0:512], Bt_s[:, :],
                                                 x2d_s[:, j * 512:(j + 1) * 512], start=True, stop=True))
                        t_smm = pe.tok()

                        @block.vector
                        def _(vector, j=j, bank=bank, t_smm=t_smm, t_decS=t_decS):
                            _w(vector, t_smm, t_decS)
                            dv.inc(vector.tensor_tensor(out=S_run[:, j * 512:(j + 1) * 512],
                                                        in0=S_run[:, j * 512:(j + 1) * 512],
                                                        in1=bank[0:128, 0:512], op=ALU.add))
                        sB_free[sbk] = dv.tok()
                    last_smm = pe.tok()
                    scan_t = dv.tok()

                @block.gpsimd
                def _(gpsimd, scan_t=scan_t):
                    _w(gpsimd, scan_t)
                    gq.inc(gpsimd.dma_start(out=E_loc.ap()[:, 0:4096], in_=S_run[:, :]), 16)
                    gpsimd.wait_ge(gq.sem, gq.n)
                    cc.inc(gpsimd.collective_compute(
                        "AllGather", ALU.bypass, replica_groups=[list(range(NCORE))],
                        ins=[E_loc.ap().opt()],
                        outs=[E_all.ap().opt()]))
                t_gather = cc.tok()

                @block.gpsimd
                def _(gpsimd):
                    _w(gpsimd, conv_done)
                    po.inc(gpsimd.tensor_copy(out=C_keep[:, :], in_=xbc_s[:, 33 * HC: 33 * HC + T]))
                t_ck = po.tok()

                if tap == "yall":
                    @block.gpsimd
                    def _(gpsimd):
                        finish(gpsimd, gq)
                        gpsimd.wait_ge(cc.sem, cc.n)
                        for jj in range(8):
                            gq.inc(gpsimd.dma_start(out=dbg[:, jj * 4096:(jj + 1) * 4096],
                                                    in_=y_dram.ap()[:, jj * 4096:(jj + 1) * 4096]), 16)
                        gpsimd.wait_ge(gq.sem, gq.n)
                    return nc
                if tap == "y":
                    @block.gpsimd
                    def _(gpsimd):
                        finish(gpsimd, gq)
                        gpsimd.wait_ge(cc.sem, cc.n)
                        gq.inc(gpsimd.dma_start(out=dbg[:, 0:4096], in_=y_dram.ap()[:, 0:4096]), 16)
                        gpsimd.wait_ge(gq.sem, gq.n)
                    return nc
                barrier()
        barrier()

        with ExitStack() as R4:
            E4 = R4.enter_context
            Eq_s = E4(nc.sbuf_tensor("Eq_s", [128, 4096], F32))
            S_core = E4(nc.sbuf_tensor("S_core", [128, 4096], F32))
            ynorm_h = E4(nc.sbuf_tensor("ynorm_h", [128, 32 * 512], BF16))
            ytile = E4(nc.sbuf_tensor("ytile", [128, 16384], F16))
            ztile = E4(nc.sbuf_tensor("ztile", [128, 16384], F16))
            agrow = E4(nc.sbuf_tensor("agrow", [65, 512], F32))
            ysq_bf = E4(nc.sbuf_tensor("ysq_bf", [128, 512], BF16))
            Cg_bf = ysq_bf
            rstd_b = E4(nc.sbuf_tensor("rstd_b", [128, 64], F32))
            rrow = E4(nc.sbuf_tensor("rrow", [1, 64], F32))
            u_tm2 = E4(nc.sbuf_tensor("u_tm2", [128, 4 * 2048], F32))
            u_sb = E4(nc.sbuf_tensor("u_sb", [128, 512], F32))
            wfo = [E4(nc.sbuf_tensor("wfo0", [128, 16 * 128], F32))] * 2
            wbo = [E4(nc.sbuf_tensor("wbo0", [128, 16 * 128], BF16))] * 2
            ares_st = [E4(nc.sbuf_tensor(f"ares_st{i}", [128, 512], F32)) for i in range(2)]
            sq_st = u_sb
            egl_bf = ares_st[0].ap().bitcast(BF16)[:, 0:512]
            J_s = E4(nc.sbuf_tensor("J_s", [64, 8], F32))
            JT_s = E4(nc.sbuf_tensor("JT_s", [8, 64], F32))
            coefm = E4(nc.sbuf_tensor("coefm", [8, 64], F32))
            S_core_bf = Eq_s.ap().bitcast(BF16)[:, 0:4096]

            @block.gpsimd
            def _(gpsimd):
                _w(gpsimd, t_gather)
                for q in range(8):
                    gq.inc(gpsimd.dma_start(out=J_s[:, q:q + 1], in_=E_all.ap()[q * 128:q * 128 + 64, 4096:4097], allow_slow_non_contiguous=True), 16)
            t_J = gq.tok()

            @block.tensor
            def _(tensor):
                _w(tensor, t_J)
                pe.inc(tensor.transpose(p_ms[0:8, 0:64], J_s[:, :], idnf_s[0:64, 0:64]))
            t_JT = pe.tok()

            @block.vector
            def _(vector):
                _w(vector, t_JT)
                dv.inc(vector.tensor_copy(out=JT_s[:, :], in_=p_ms[0:8, 0:64]))
            t_JTs = dv.tok()

            @block.tensor
            def _(tensor):
                _w(tensor, t_JTs)
                pe.inc(tensor.matmul(p_bc[0:8, 0:64], wmaskT_s[:, :], JT_s[:, :], start=True, stop=True))
            t_Lq = pe.tok()

            @block.scalar
            def _(scalar):
                _w(scalar, t_Lq)
                ac.inc(scalar.activation(out=coefm[:, :], in_=p_bc[0:8, 0:64], func=AF.Exp))
            t_cf1 = ac.tok()

            @block.vector
            def _(vector):
                _w(vector, t_cf1)
                dv.inc(vector.tensor_scalar(out=coefm[:, :], in0=coefm[:, :],
                                            scalar1=qmask_s[:, :], scalar2=None, op0=ALU.mult))
            t_cf2 = dv.tok()

            @block.gpsimd
            def _(gpsimd):
                _w(gpsimd, t_cf2)
                gq.inc(gpsimd.dma_start(out=coef_dram.ap(), in_=coefm[:, :]), 16)
                gpsimd.wait_ge(gq.sem, gq.n)
                gq.inc(gpsimd.dma_start(out=agrow[0:1, :],
                                        in_=coef_dram.ap().rearrange("q h -> (q h)").unsqueeze(0)), 16)
            t_cfl = gq.tok()

            @block.vector
            def _(vector):
                _w(vector, t_cfl)
                dv.inc(vector.memset(S_core[:, :], 0.0))
            t_cbe = dv.tok()

            eq_t = None
            for q in range(8):
                @block.gpsimd
                def _(gpsimd, q=q, eq_t=eq_t):
                    _w(gpsimd, t_gather, eq_t)
                    gq.inc(gpsimd.dma_start(out=Eq_s[:, :], in_=E_all.ap()[q * 128:(q + 1) * 128, 0:4096]), 16)
                t_eq = gq.tok()

                @block.tensor
                def _(tensor, q=q, eq_t=eq_t):
                    _w(tensor, t_cfl, eq_t)
                    pe.inc(tensor.matmul(p_bc[0:128, 0:64], ones_row[:, :],
                                         agrow[0:1, q * 64:(q + 1) * 64], start=True, stop=True))
                t_cq = pe.tok()

                @block.vector
                def _(vector, q=q, t_eq=t_eq, t_cq=t_cq):
                    _w(vector, t_eq, t_cbe, t_cq)
                    dv.inc(vector.tensor_tensor(
                        out=Eq_s.ap().rearrange("p (h w) -> p h w", h=64),
                        in0=Eq_s.ap().rearrange("p (h w) -> p h w", h=64),
                        in1=p_bc[0:128, 0:64, None].broadcast_to([128, 64, 64]), op=ALU.mult))
                    dv.inc(vector.tensor_tensor(out=S_core[:, :], in0=S_core[:, :], in1=Eq_s[:, :],
                                                op=ALU.add))
                eq_t = dv.tok()

            @block.gpsimd
            def _(gpsimd, eq_t=eq_t):
                _w(gpsimd, eq_t, t_ck)
                po.inc(gpsimd.tensor_copy(out=S_core_bf, in_=S_core[:, :]))
            t_scbf = po.tok()

            tile_done = None
            wodma_t = [None, None]
            wocast_t = [None, None]
            woused_t = [None, None]
            stat_t = None
            ares_free = [None, None]
            corr_free = [None, None]
            corr_free[0] = t_cbe
            corr_free[1] = t_JTs
            uacc_free = None
            for tt in range(2):
                @block.gpsimd
                def _(gpsimd, tt=tt, tile_done=tile_done):
                    _w(gpsimd, ydma_t[0], ydma_t[1], tile_done)
                    gq.inc(gpsimd.dma_start(out=ytile[:, :],
                                            in_=y_dram.ap()[:, tt * 16384:(tt + 1) * 16384]), 16)
                t_yt = gq.tok()

                @block.sync
                def _(sync, tt=tt, tile_done=tile_done):
                    _w(sync, tile_done)
                    sync.wait_ge(gq.sem, z_gq_done[1])
                    for g in range(32):
                        sq.inc(sync.dma_start(out=ztile[:, g * 512:(g + 1) * 512],
                                              in_=z_dram.ap()[:, g * T + tt * 512: g * T + tt * 512 + 512]), 16)
                t_zt = sq.tok()

                hcorr_t = eq_t
                hcg_t = None
                for h in range(NH):
                    par = h % 2
                    arow = 64 * par

                    @block.sync
                    def _(sync, tt=tt, h=h, arow=arow, hcorr_t=hcorr_t):
                        _w(sync, hcorr_t)
                        sq.inc(sync.dma_start(out=agrow[arow:arow + 1, :],
                                              in_=aglob_dram.ap()[h:h + 1, tt * 512:(tt + 1) * 512]), 16)
                    t_ag = sq.tok()

                    @block.tensor
                    def _(tensor, arow=arow, t_ag=t_ag):
                        _w(tensor, t_ag, corr_free[0])
                        pe.inc(tensor.matmul(p_bc[0:128, 0:512], ones65[arow:arow + 1, :],
                                             agrow[arow:arow + 1, :], start=True, stop=True))
                    t_agb = pe.tok()

                    @block.scalar
                    def _(scalar, t_agb=t_agb, hcg_t=hcg_t):
                        _w(scalar, t_agb, hcg_t)
                        ac.inc(scalar.activation(out=egl_bf[:, :], in_=p_bc[:, 0:512], func=AF.Exp))
                    t_egl = ac.tok()
                    corr_free[0] = t_egl

                    @block.vector
                    def _(vector, tt=tt, t_egl=t_egl, hcorr_t=hcorr_t):
                        _w(vector, t_egl, hcorr_t)
                        dv.inc(vector.tensor_tensor(out=Cg_bf[:, :], in0=egl_bf[:, :],
                                                    in1=C_keep[:, tt * 512:(tt + 1) * 512], op=ALU.mult))
                    hcg_t = dv.tok()

                    @block.tensor
                    def _(tensor, h=h, hcg_t=hcg_t):
                        _w(tensor, hcg_t, t_scbf, corr_free[1])
                        base = (h % 2) * 64
                        pe.inc(tensor.matmul(p_ms[base:base + 64, 0:512], S_core_bf[:, h * 64:(h + 1) * 64],
                                             Cg_bf[:, :], start=True, stop=True))
                    t_corr = pe.tok()

                    @block.vector
                    def _(vector, h=h, t_corr=t_corr, t_yt=t_yt):
                        _w(vector, t_corr, t_yt)
                        g = h // 2
                        base = (h % 2) * 64
                        yv = ytile[base:base + 64, :].rearrange("p (c gg t) -> p c gg t",
                                                                c=8, gg=32)[:, :, g, :]
                        ins = None
                        for cc_ in range(8):
                            yv2 = ytile[base:base + 64, cc_ * 2048 + g * 64: cc_ * 2048 + g * 64 + 64]
                            ins = vector.tensor_tensor(
                                out=yv2, in0=yv2,
                                in1=p_ms[base:base + 64, cc_ * 64:(cc_ + 1) * 64], op=ALU.add)
                        dv.inc(ins)
                    hcorr_t = dv.tok()
                    corr_free[1] = hcorr_t

                if tap == "ycorr" and tt == 0:
                    @block.gpsimd
                    def _(gpsimd):
                        finish(gpsimd, gq)
                        gq.inc(gpsimd.dma_start(out=dbg[:, 0:4096], in_=ytile[:, 0:4096]), 16)
                        gq.inc(gpsimd.dma_start(out=dbg[:, 4096:8192], in_=y_dram.ap()[:, 0:4096]), 16)
                        gpsimd.wait_ge(gq.sem, gq.n)
                    return nc

                if tap == "corr2" and tt == 1:
                    @block.gpsimd
                    def _(gpsimd):
                        finish(gpsimd, gq)
                        gq.inc(gpsimd.dma_start(out=dbg[:, 0:2048], in_=ytile[:, 0:2048]), 16)
                        gpsimd.wait_ge(gq.sem, gq.n)
                    return nc

                prev_rr = tile_done
                prev_rbe = tile_done
                prev_sq = tile_done
                for ci in range(8):
                    c = tt * 8 + ci

                    @block.vector
                    def _(vector, ci=ci, hcorr_t=hcorr_t, t_zt=t_zt):
                        _w(vector, hcorr_t, t_zt)
                        yv = ytile[:, ci * 2048:(ci + 1) * 2048].rearrange("p (g t) -> p g t", g=32)
                        zv = ztile.ap().rearrange("p (g t) -> p g t", g=32)[:, :, ci * 64:(ci + 1) * 64]
                        dv.inc(vector.tensor_tensor(out=yv, in0=yv, in1=zv, op=ALU.mult))
                    t_gate = dv.tok()

                    t_ss = None
                    for sub in range(4):
                        @block.scalar
                        def _(scalar, ci=ci, sub=sub, t_gate=t_gate, prev_sq=prev_sq, t_ss=t_ss):
                            _w(scalar, t_gate, prev_sq if sub == 0 else None, t_ss)
                            ac.inc(scalar.activation(
                                out=ysq_bf[:, :],
                                in_=ytile[:, ci * 2048 + sub * 512: ci * 2048 + sub * 512 + 512],
                                func=AF.Square))
                        t_ysq = ac.tok()

                        @block.tensor
                        def _(tensor, sub=sub, t_ysq=t_ysq, prev_rr=prev_rr):
                            _w(tensor, t_ysq, prev_rr if sub == 0 else None)
                            for g in range(8):
                                pe.inc(tensor.matmul(p_io0[0:1, 0:64], ones_col_b[:, :],
                                                     ysq_bf[:, g * 64:(g + 1) * 64],
                                                     start=(sub == 0 and g == 0), stop=(sub == 3 and g == 7)),
                                       1 if (sub == 3 and g == 7) else 0)
                        t_ss = pe.tok()
                    prev_sq = t_ss

                    @block.scalar
                    def _(scalar, t_ss=t_ss):
                        _w(scalar, t_ss)
                        ac.inc(scalar.activation(out=rrow[:, :], in_=p_io0[0:1, 0:64], func=AF.Ln,
                                                 scale=1.0 / DIN, bias=eps1[:, :]))
                        ac.inc(scalar.activation(out=rrow[:, :], in_=rrow[:, :], func=AF.Exp, scale=-0.5))
                    t_rr = ac.tok()
                    prev_rr = t_rr

                    @block.tensor
                    def _(tensor, t_rr=t_rr, prev_rbe=prev_rbe):
                        _w(tensor, t_rr, prev_rbe)
                        pe.inc(tensor.matmul(p_io1[0:128, 0:64], ones_row[:, :], rrow[:, :],
                                             start=True, stop=True))
                    t_rb = pe.tok()

                    @block.vector
                    def _(vector, t_rb=t_rb):
                        _w(vector, t_rb)
                        dv.inc(vector.tensor_copy(out=rstd_b[:, :], in_=p_io1[0:128, 0:64]))
                    t_rbe = dv.tok()
                    prev_rbe = t_rbe

                    @block.vector
                    def _(vector, ci=ci, t_rbe=t_rbe):
                        _w(vector, t_rbe)
                        yv = ytile[:, ci * 2048:(ci + 1) * 2048].rearrange("p (g t) -> p g t", g=32)
                        ov = ynorm_h.ap().rearrange("p (g tl) -> p g tl", g=32)[:, :, ci * 64:(ci + 1) * 64]
                        dv.inc(vector.tensor_tensor(out=yv, in0=yv,
                                                    in1=rstd_b[:, None, :].broadcast_to([128, 32, 64]),
                                                    op=ALU.mult))
                        dv.inc(vector.tensor_tensor(out=ov, in0=yv,
                                                    in1=nw_s[:, :, None].broadcast_to([128, 32, 64]),
                                                    op=ALU.mult))
                t_yn = dv.tok()

                if tap == "ynorm" and tt == 0:
                    @block.gpsimd
                    def _(gpsimd):
                        finish(gpsimd, gq)
                        gq.inc(gpsimd.dma_start(out=dbg[:, 0:2048], in_=ynorm_h[:, 0:2048]), 16)
                        gpsimd.wait_ge(gq.sem, gq.n)
                    return nc

                ares_free = [prev_rr, prev_rbe]
                @block.sync
                def _(sync, tt=tt, tile_done=tile_done):
                    _w(sync, tile_done)
                    sq.inc(sync.dma_start(out=u_tm2[:, :].rearrange("p (a d) -> p a d", a=4),
                                          in_=u_pad[3 + tt * 512: 3 + tt * 512 + 512, :].rearrange(
                                              "(a p) d -> p a d", p=128)), 16)
                t_u2 = sq.tok()

                for m in range(16):
                    par = m % 2
                    bank = [p_io0, p_io1][par]

                    @block.sync
                    def _(sync, m=m):
                        _w(sync, wocast_t[0])
                        src = WoutT.rearrange("(kc kp) mm -> kp kc mm", kp=128)[:, 0:16, m * 128:(m + 1) * 128]
                        dst = wfo[0][:, :].rearrange("p (kc mm) -> p kc mm", kc=16)
                        sq.inc(sync.dma_start(out=dst, in_=src), 16)
                    wodma_t[0] = sq.tok()

                    for kk in range(2):
                        if kk == 1:
                            @block.sync
                            def _(sync, m=m):
                                _w(sync, wocast_t[0])
                                src = WoutT.rearrange("(kc kp) mm -> kp kc mm",
                                                      kp=128)[:, 16:32, m * 128:(m + 1) * 128]
                                dst = wfo[0][:, :].rearrange("p (kc mm) -> p kc mm", kc=16)
                                sq.inc(sync.dma_start(out=dst, in_=src), 16)
                            wodma_t[0] = sq.tok()

                        @block.gpsimd
                        def _(gpsimd):
                            _w(gpsimd, wodma_t[0], woused_t[0])
                            po.inc(gpsimd.tensor_copy(out=wbo[0][:, :], in_=wfo[0][:, :]))
                        wocast_t[0] = po.tok()

                        @block.tensor
                        def _(tensor, m=m, par=par, kk=kk, bank=bank, t_yn=t_yn):
                            _w(tensor, wocast_t[0], t_yn)
                            if kk == 0:
                                _w(tensor, ares_free[par])
                            for k in range(16):
                                pe.inc(tensor.matmul(bank[:, 0:512], wbo[0][:, k * 128:(k + 1) * 128],
                                                     ynorm_h[:, (kk * 16 + k) * 512: (kk * 16 + k) * 512 + 512],
                                                     start=(kk == 0 and k == 0), stop=(kk == 1 and k == 15)),
                                       1 if k == 15 else 0)
                        woused_t[0] = pe.tok()
                    t_wo = pe.tok()

                    @block.tensor
                    def _(tensor, m=m, t_u2=t_u2, uacc_free=uacc_free):
                        _w(tensor, t_u2, uacc_free)
                        for j in range(4):
                            pe.inc(tensor.transpose(p_y0[0:128, j * 128:(j + 1) * 128],
                                                    u_tm2[:, j * 2048 + m * 128: j * 2048 + m * 128 + 128],
                                                    idnf_s[:, :]))
                    t_utp = pe.tok()

                    @block.scalar
                    def _(scalar, t_utp=t_utp, stat_t=stat_t):
                        _w(scalar, t_utp, stat_t)
                        ac.inc(scalar.activation(out=u_sb[:, :], in_=p_y0[:, 0:512], func=AF.Copy))
                    t_usb = ac.tok()
                    uacc_free = t_usb

                    @block.vector
                    def _(vector, m=m, par=par, bank=bank, t_wo=t_wo, t_usb=t_usb):
                        _w(vector, t_wo, t_usb, ares_free[par])
                        dv.inc(vector.tensor_tensor(out=ares_st[par][:, :], in0=bank[:, 0:512],
                                                    in1=u_sb[:, :], op=ALU.add))
                    t_ares = dv.tok()

                    @block.gpsimd
                    def _(gpsimd, m=m, tt=tt, par=par, t_ares=t_ares):
                        _w(gpsimd, t_ares)
                        gq.inc(gpsimd.dma_start(
                            out=ares_dram.ap()[:, m * T + tt * 512: m * T + tt * 512 + 512],
                            in_=ares_st[par][:, :]), 16)
                    t_adma = gq.tok()

                    @block.scalar
                    def _(scalar, par=par, t_ares=t_ares, stat_t=stat_t):
                        _w(scalar, t_ares, stat_t)
                        ac.inc(scalar.activation(out=sq_st[:, :], in_=ares_st[par][:, :], func=AF.Square))
                    t_sqs = ac.tok()

                    @block.tensor
                    def _(tensor, par=par, t_sqs=t_sqs, stat_t=stat_t):
                        _w(tensor, t_sqs, stat_t)
                        pe.inc(tensor.matmul(p_ms[0:1, 0:512], ones_col[:, :], ares_st[par][:, :],
                                             start=True, stop=True))
                        pe.inc(tensor.matmul(p_bc[0:1, 0:512], ones_col[:, :], sq_st[:, :],
                                             start=True, stop=True))
                    t_st = pe.tok()

                    @block.vector
                    def _(vector, m=m, tt=tt, t_st=t_st, t_adma=t_adma):
                        _w(vector, t_st, t_adma)
                        if m == 0:
                            dv.inc(vector.tensor_copy(out=stat_row[0:1, tt * 512:(tt + 1) * 512],
                                                      in_=p_ms[0:1, 0:512]))
                            dv.inc(vector.tensor_copy(out=stat_row2[0:1, tt * 512:(tt + 1) * 512],
                                                      in_=p_bc[0:1, 0:512]))
                        else:
                            dv.inc(vector.tensor_tensor(out=stat_row[0:1, tt * 512:(tt + 1) * 512],
                                                        in0=stat_row[0:1, tt * 512:(tt + 1) * 512],
                                                        in1=p_ms[0:1, 0:512], op=ALU.add))
                            dv.inc(vector.tensor_tensor(out=stat_row2[0:1, tt * 512:(tt + 1) * 512],
                                                        in0=stat_row2[0:1, tt * 512:(tt + 1) * 512],
                                                        in1=p_bc[0:1, 0:512], op=ALU.add))
                    stat_t = dv.tok()
                    ares_free[par] = stat_t
                    corr_free[0] = stat_t
                    corr_free[1] = stat_t
                tile_done = dv.tok()
                if tap == "wout2" and tt == 1:
                    @block.gpsimd
                    def _(gpsimd):
                        finish(gpsimd, gq)
                        gq.inc(gpsimd.dma_start(out=dbg[:, 0:1024], in_=ares_dram.ap()[:, 0:1024]), 16)
                        gpsimd.wait_ge(gq.sem, gq.n)
                    return nc
                if tap == "wout" and tt == 0:
                    @block.gpsimd
                    def _(gpsimd):
                        finish(gpsimd, gq)
                        gq.inc(gpsimd.dma_start(out=dbg[:, 0:1024], in_=ares_dram.ap()[:, 0:1024]), 16)
                        gq.inc(gpsimd.dma_start(out=dbg[:, 1024:2048], in_=ares_dram.ap()[:, 1024:2048]), 16)
                        gpsimd.wait_ge(gq.sem, gq.n)
                    return nc
            t_p5 = dv.tok()

            @block.vector
            def _(vector):
                _w(vector, t_p5)
                dv.inc(vector.tensor_scalar(out=stat_row[0:1, :], in0=stat_row[0:1, :],
                                            scalar1=1.0 / D, scalar2=None, op0=ALU.mult))
                dv.inc(vector.tensor_scalar(out=stat_row2[0:1, :], in0=stat_row2[0:1, :],
                                            scalar1=1.0 / D, scalar2=None, op0=ALU.mult))
                dv.inc(vector.tensor_tensor(out=ln_scr[0:1, :], in0=stat_row[0:1, :],
                                            in1=stat_row[0:1, :], op=ALU.mult))
                dv.inc(vector.tensor_tensor(out=stat_row2[0:1, :], in0=stat_row2[0:1, :],
                                            in1=ln_scr[0:1, :], op=ALU.subtract))
            t_v1 = dv.tok()
            if tap == "p5a":
                @block.gpsimd
                def _(gpsimd):
                    finish(gpsimd, gq)
                    gq.inc(gpsimd.dma_start(out=dbg[0:1, 0:1024], in_=stat_row[0:1, :]), 16)
                    gq.inc(gpsimd.dma_start(out=dbg[0:1, 1024:2048], in_=stat_row2[0:1, :]), 16)
                    gpsimd.wait_ge(gq.sem, gq.n)
                return nc

            @block.scalar
            def _(scalar):
                _w(scalar, t_v1)
                ac.inc(scalar.activation(out=stat_row2[0:1, :], in_=stat_row2[0:1, :], func=AF.Ln,
                                         bias=eps1[:, :]))
                ac.inc(scalar.activation(out=stat_row2[0:1, :], in_=stat_row2[0:1, :], func=AF.Exp,
                                         scale=-0.5))
            t_r1 = ac.tok()
            if tap == "p5b":
                @block.gpsimd
                def _(gpsimd):
                    finish(gpsimd, gq)
                    gq.inc(gpsimd.dma_start(out=dbg[0:1, 0:1024], in_=stat_row[0:1, :]), 16)
                    gq.inc(gpsimd.dma_start(out=dbg[0:1, 1024:2048], in_=stat_row2[0:1, :]), 16)
                    gpsimd.wait_ge(gq.sem, gq.n)
                return nc

            @block.vector
            def _(vector):
                _w(vector, t_r1)
                dv.inc(vector.tensor_tensor(out=stat_row[0:1, :], in0=stat_row[0:1, :],
                                            in1=stat_row2[0:1, :], op=ALU.mult))
                dv.inc(vector.tensor_scalar(out=stat_row[0:1, :], in0=stat_row[0:1, :],
                                            scalar1=-1.0, scalar2=None, op0=ALU.mult))
            t_am = dv.tok()

            if tap == "p5c":
                @block.gpsimd
                def _(gpsimd):
                    finish(gpsimd, gq)
                    gq.inc(gpsimd.dma_start(out=dbg[0:1, 0:1024], in_=stat_row[0:1, :]), 16)
                    gpsimd.wait_ge(gq.sem, gq.n)
                return nc

            bc_prev = None
            for half in range(2):
                @block.tensor
                def _(tensor, half=half, bc_prev=bc_prev):
                    _w(tensor, t_am, bc_prev)
                    pe.inc(tensor.matmul(p_io0[:, 0:512], ones_row[:, :],
                                         stat_row2[0:1, half * 512:(half + 1) * 512], start=True, stop=True))
                    pe.inc(tensor.matmul(p_io1[:, 0:512], ones_row[:, :],
                                         stat_row[0:1, half * 512:(half + 1) * 512], start=True, stop=True))
                t_b2 = pe.tok()

                @block.vector
                def _(vector, half=half, t_b2=t_b2):
                    _w(vector, t_b2)
                    dv.inc(vector.tensor_copy(out=rstd1_b[:, half * 512:(half + 1) * 512],
                                              in_=p_io0[:, 0:512]))
                    dv.inc(vector.tensor_copy(out=am1_b[:, half * 512:(half + 1) * 512],
                                              in_=p_io1[:, 0:512]))
                bc_prev = dv.tok()
                if tap == "p5d" and half == 0:
                    @block.gpsimd
                    def _(gpsimd):
                        finish(gpsimd, gq)
                        gq.inc(gpsimd.dma_start(out=dbg[:, 0:1024], in_=rstd1_b[:, :]), 16)
                        gpsimd.wait_ge(gq.sem, gq.n)
                    return nc
            t_ln1 = dv.tok()
            if tap == "p5":
                @block.gpsimd
                def _(gpsimd):
                    finish(gpsimd, gq)
                    gq.inc(gpsimd.dma_start(out=dbg[:, 0:1024], in_=rstd1_b[:, :]), 16)
                    gq.inc(gpsimd.dma_start(out=dbg[:, 1024:2048], in_=am1_b[:, :]), 16)
                    gpsimd.wait_ge(gq.sem, gq.n)
                return nc
            barrier()

        with ExitStack() as R56:
            E56 = R56.enter_context
            g_res = E56(nc.sbuf_tensor("g_res", [128, 43 * T], BF16))
            ar_ld = [E56(nc.sbuf_tensor(f"ar_ld{i}", [128, T], F32)) for i in range(2)]

            with ExitStack() as R5:
                E5 = R5.enter_context
                x1bf = E5(nc.sbuf_tensor("x1bf", [128, 16 * T], BF16))
                wq = [E5(nc.sbuf_tensor(f"wq{i}", [128, 2048], F32)) for i in range(4)]
                q1_bf = [E5(nc.sbuf_tensor(f"q1_bf{i}", [128, 2048], BF16)) for i in range(2)]
                q3_bf = [E5(nc.sbuf_tensor(f"q3_bf{i}", [128, 2048], BF16)) for i in range(2)]
                qs_bf = [E5(nc.sbuf_tensor("qs_bf0", [128, 2048], BF16))] * 2
                sh1_bf = E5(nc.sbuf_tensor("sh1_bf", [128, 512], BF16))

                ar_t = [None, None]
                x1_t = [None, None]
                for k in range(16):
                    par = k % 2

                    @block.sync
                    def _(sync, k=k, par=par):
                        _w(sync, x1_t[par])
                        sq.inc(sync.dma_start(out=ar_ld[par][:, :],
                                              in_=ares_dram.ap()[:, k * T:(k + 1) * T]), 16)
                    ar_t[par] = sq.tok()

                    @block.vector
                    def _(vector, k=k, par=par):
                        _w(vector, ar_t[par], t_ln1)
                        dv.inc(vector.tensor_tensor(out=ar_ld[par][:, :], in0=ar_ld[par][:, :],
                                                    in1=rstd1_b[:, :], op=ALU.mult))
                        dv.inc(vector.tensor_tensor(out=x1bf[:, k * T:(k + 1) * T], in0=ar_ld[par][:, :],
                                                    in1=am1_b[:, :], op=ALU.add))
                    x1_t[par] = dv.tok()
                t_x1 = dv.tok()

                if tap == "x1":
                    @block.gpsimd
                    def _(gpsimd):
                        finish(gpsimd, gq)
                        gq.inc(gpsimd.dma_start(out=dbg[:, 0:1024], in_=x1bf[:, 0:1024]), 16)
                        gq.inc(gpsimd.dma_start(out=dbg[:, 1024:2048], in_=x1bf[:, 1024:2048]), 16)
                        gpsimd.wait_ge(gq.sem, gq.n)
                    return nc

                wdma2_t = [None, None]
                q_t = [None, None]
                qused_t = [None, None]
                hio_free = [None, None, None, None]
                HB = [p_io0, p_io1, p_bc, p_ms]
                g_t = None
                for m in range(43):
                    par = m % 2
                    mw = 128 if m < 42 else 85

                    @block.sync
                    def _(sync, m=m, par=par, mw=mw):
                        _w(sync, q_t[par])
                        for wsrc, slot in ((w1T, 0), (w3T, 1)):
                            src = wsrc.rearrange("(kc kp) mm -> kp kc mm", kp=128)[:, :, m * 128:m * 128 + mw]
                            dst = wq[slot * 2 + par][:, 0:16 * mw].rearrange("p (kc mm) -> p kc mm", kc=16)
                            sq.inc(sync.dma_start(out=dst, in_=src), 16)
                    wdma2_t[par] = sq.tok()

                    @block.scalar
                    def _(scalar, par=par, mw=mw):
                        _w(scalar, wdma2_t[par], t_shalf, qused_t[par])
                        ac.inc(scalar.activation(out=q1_bf[par][:, 0:16 * mw], in_=wq[par][:, 0:16 * mw],
                                                 func=AF.Sign, bias=sneg_s[:, 0:1]))
                        ac.inc(scalar.activation(out=qs_bf[par][:, 0:16 * mw], in_=wq[par][:, 0:16 * mw],
                                                 func=AF.Sign, bias=sneg_s[:, 0:1], scale=-1.0))
                    t_sg1 = ac.tok()

                    @block.vector
                    def _(vector, par=par, mw=mw, t_sg1=t_sg1):
                        _w(vector, t_sg1)
                        dv.inc(vector.tensor_tensor(out=q1_bf[par][:, 0:16 * mw],
                                                    in0=q1_bf[par][:, 0:16 * mw],
                                                    in1=qs_bf[par][:, 0:16 * mw], op=ALU.subtract))
                    t_q1 = dv.tok()

                    @block.scalar
                    def _(scalar, par=par, mw=mw, t_q1=t_q1):
                        _w(scalar, t_q1)
                        ac.inc(scalar.activation(out=qs_bf[par][:, 0:16 * mw], in_=wq[2 + par][:, 0:16 * mw],
                                                 func=AF.Sign, bias=sneg_s[:, 2:3]))
                        ac.inc(scalar.activation(out=q3_bf[par][:, 0:16 * mw], in_=wq[2 + par][:, 0:16 * mw],
                                                 func=AF.Sign, bias=sneg_s[:, 2:3], scale=-1.0))
                    t_sg3 = ac.tok()

                    @block.vector
                    def _(vector, par=par, mw=mw, t_sg3=t_sg3):
                        _w(vector, t_sg3)
                        dv.inc(vector.tensor_tensor(out=q3_bf[par][:, 0:16 * mw],
                                                    in0=qs_bf[par][:, 0:16 * mw],
                                                    in1=q3_bf[par][:, 0:16 * mw], op=ALU.subtract))
                    q_t[par] = dv.tok()

                    for nt in range(2):
                        b1 = HB[nt * 2]
                        b3 = HB[nt * 2 + 1]

                        @block.tensor
                        def _(tensor, m=m, par=par, mw=mw, nt=nt, b1=b1, b3=b3):
                            _w(tensor, q_t[par], t_x1, hio_free[nt * 2], hio_free[nt * 2 + 1])
                            for k in range(16):
                                pe.inc(tensor.matmul(b1[0:mw, 0:512], q1_bf[par][:, k * mw: k * mw + mw],
                                                     x1bf[:, k * T + nt * 512: k * T + nt * 512 + 512],
                                                     start=(k == 0), stop=(k == 15)), 1 if k == 15 else 0)
                            for k in range(16):
                                pe.inc(tensor.matmul(b3[0:mw, 0:512], q3_bf[par][:, k * mw: k * mw + mw],
                                                     x1bf[:, k * T + nt * 512: k * T + nt * 512 + 512],
                                                     start=(k == 0), stop=(k == 15)), 1 if k == 15 else 0)
                        t_h = pe.tok()
                        qused_t[par] = t_h

                        @block.scalar
                        def _(scalar, mw=mw, b1=b1, t_h=t_h, g_t=g_t):
                            _w(scalar, t_h, g_t)
                            ac.inc(scalar.activation(out=sh1_bf[0:mw, :], in_=b1[0:mw, 0:512],
                                                     func=AF.Silu, scale=shalf_s[0:mw, 0:1]))
                        t_sh1 = ac.tok()

                        @block.vector
                        def _(vector, m=m, mw=mw, nt=nt, b3=b3, t_sh1=t_sh1):
                            _w(vector, t_sh1)
                            dv.inc(vector.scalar_tensor_tensor(
                                out=g_res[0:mw, m * T + nt * 512: m * T + nt * 512 + 512],
                                in0=b3[0:mw, 0:512], scalar=shalf_s[0:mw, 2:3], in1=sh1_bf[0:mw, :],
                                op0=ALU.mult, op1=ALU.mult))
                        g_t = dv.tok()
                        hio_free[nt * 2] = g_t
                        hio_free[nt * 2 + 1] = g_t
                t_g = dv.tok()

                if tap == "g":
                    @block.gpsimd
                    def _(gpsimd):
                        finish(gpsimd, gq)
                        gq.inc(gpsimd.dma_start(out=dbg[:, 0:1024], in_=g_res[:, 0:1024]), 16)
                        gq.inc(gpsimd.dma_start(out=dbg[:, 1024:2048], in_=g_res[:, 42 * T:42 * T + 1024]), 16)
                        gpsimd.wait_ge(gq.sem, gq.n)
                    return nc
                barrier()

            with ExitStack() as R6:
                E6 = R6.enter_context
                wq2 = E6(nc.sbuf_tensor("w2buf", [128, 43 * 128], F32))
                q2_bf = E6(nc.sbuf_tensor("q2bbuf", [128, 43 * 128], BF16))
                qs2_bf = E6(nc.sbuf_tensor("qs2buf", [128, 43 * 128], BF16))
                x1f_st = E6(nc.sbuf_tensor("x1f_st", [128, T], F32))
                x2_st = [E6(nc.sbuf_tensor(f"x2_st{i}", [128, 512], F32)) for i in range(2)]
                sq2_st = E6(nc.sbuf_tensor("sq2_st", [128, 512], F32))

                w2dma_t = None
                q2_t = None
                q2used_t = None
                ar2_t = [None, None]
                x2_free = [None, None]
                st2_t = None
                for m in range(16):
                    par = m % 2

                    @block.sync
                    def _(sync, m=m, par=par, q2_t=q2_t):
                        _w(sync, q2_t)
                        if m < 2:
                            _w(sync, t_g)
                        src = w2T[0:5376, m * 128:(m + 1) * 128].rearrange("(kc kp) mm -> kp kc mm", kp=128)
                        dst = wq2[:, 0:42 * 128].rearrange("p (kc mm) -> p kc mm", kc=42)
                        sq.inc(sync.dma_start(out=dst, in_=src), 16)
                        sq.inc(sync.dma_start(out=wq2[0:85, 42 * 128:43 * 128],
                                              in_=w2T[5376:5461, m * 128:(m + 1) * 128]), 16)
                        _w(sync, ar2_t[par])
                        sq.inc(sync.dma_start(out=ar_ld[par][:, :],
                                              in_=ares_dram.ap()[:, m * T:(m + 1) * T]), 16)
                    w2dma_t = sq.tok()

                    @block.scalar
                    def _(scalar, w2dma_t=w2dma_t, q2used_t=q2used_t):
                        _w(scalar, w2dma_t, q2used_t)
                        ac.inc(scalar.activation(out=q2_bf[:, :], in_=wq2[:, :],
                                                 func=AF.Sign, bias=sneg_s[:, 1:2]))
                        ac.inc(scalar.activation(out=qs2_bf[:, :], in_=wq2[:, :],
                                                 func=AF.Sign, bias=sneg_s[:, 1:2], scale=-1.0))
                    t_sg2 = ac.tok()

                    @block.vector
                    def _(vector, par=par, t_sg2=t_sg2):
                        _w(vector, t_sg2)
                        dv.inc(vector.tensor_tensor(out=q2_bf[:, :], in0=q2_bf[:, :],
                                                    in1=qs2_bf[:, :], op=ALU.subtract))
                        dv.inc(vector.tensor_tensor(out=ar_ld[par][:, :], in0=ar_ld[par][:, :],
                                                    in1=rstd1_b[:, :], op=ALU.mult))
                        dv.inc(vector.tensor_tensor(out=x1f_st[:, :], in0=ar_ld[par][:, :],
                                                    in1=am1_b[:, :], op=ALU.add))
                    q2_t = dv.tok()
                    ar2_t[par] = q2_t

                    for nt in range(2):
                        bank = [p_io0, p_io1][nt]

                        @block.tensor
                        def _(tensor, nt=nt, bank=bank, q2_t=q2_t):
                            _w(tensor, q2_t, t_g, x2_free[nt])
                            for k in range(43):
                                kw = 128 if k < 42 else 85
                                pe.inc(tensor.matmul(bank[:, 0:512], q2_bf[0:kw, k * 128:k * 128 + 128],
                                                     g_res[0:kw, k * T + nt * 512: k * T + nt * 512 + 512],
                                                     start=(k == 0), stop=(k == 42)), 1 if k == 42 else 0)
                        t_o = pe.tok()
                        q2used_t = t_o

                        @block.vector
                        def _(vector, nt=nt, bank=bank, t_o=t_o, q2_t=q2_t):
                            _w(vector, t_o)
                            dv.inc(vector.scalar_tensor_tensor(
                                out=x2_st[nt][:, :],
                                in0=bank[:, 0:512], scalar=shalf_s[:, 1:2],
                                in1=x1f_st[:, nt * 512:(nt + 1) * 512],
                                op0=ALU.mult, op1=ALU.add))
                        t_x2p = dv.tok()

                        @block.gpsimd
                        def _(gpsimd, m=m, nt=nt, t_x2p=t_x2p):
                            _w(gpsimd, t_x2p)
                            gq.inc(gpsimd.dma_start(
                                out=x2_dram.ap()[:, m * T + nt * 512: m * T + nt * 512 + 512],
                                in_=x2_st[nt][:, :]), 16)
                        t_x2dma = gq.tok()

                        @block.scalar
                        def _(scalar, nt=nt, t_x2p=t_x2p, st2_t=st2_t):
                            _w(scalar, t_x2p, st2_t)
                            ac.inc(scalar.activation(out=sq2_st[:, :], in_=x2_st[nt][:, :], func=AF.Square))
                        t_sq2 = ac.tok()

                        @block.tensor
                        def _(tensor, nt=nt, t_sq2=t_sq2, st2_t=st2_t):
                            _w(tensor, t_sq2, st2_t)
                            pe.inc(tensor.matmul(p_ms[0:1, 0:512], ones_col[:, :], x2_st[nt][:, :],
                                                 start=True, stop=True))
                            pe.inc(tensor.matmul(p_bc[0:1, 0:512], ones_col[:, :], sq2_st[:, :],
                                                 start=True, stop=True))
                        t_st2 = pe.tok()

                        @block.vector
                        def _(vector, m=m, nt=nt, t_st2=t_st2, t_x2dma=t_x2dma):
                            _w(vector, t_st2, t_x2dma)
                            tc = nt * 512
                            if m == 0:
                                dv.inc(vector.tensor_copy(out=stat_row[0:1, tc:tc + 512],
                                                          in_=p_ms[0:1, 0:512]))
                                dv.inc(vector.tensor_copy(out=stat_row2[0:1, tc:tc + 512],
                                                          in_=p_bc[0:1, 0:512]))
                            else:
                                dv.inc(vector.tensor_tensor(out=stat_row[0:1, tc:tc + 512],
                                                            in0=stat_row[0:1, tc:tc + 512],
                                                            in1=p_ms[0:1, 0:512], op=ALU.add))
                                dv.inc(vector.tensor_tensor(out=stat_row2[0:1, tc:tc + 512],
                                                            in0=stat_row2[0:1, tc:tc + 512],
                                                            in1=p_bc[0:1, 0:512], op=ALU.add))
                        st2_t = dv.tok()
                        x2_free[nt] = st2_t
                t_w2done = dv.tok()
                barrier()

            @block.vector
            def _(vector):
                _w(vector, t_w2done)
                dv.inc(vector.tensor_scalar(out=stat_row[0:1, :], in0=stat_row[0:1, :],
                                            scalar1=1.0 / D, scalar2=None, op0=ALU.mult))
                dv.inc(vector.tensor_scalar(out=stat_row2[0:1, :], in0=stat_row2[0:1, :],
                                            scalar1=1.0 / D, scalar2=None, op0=ALU.mult))
                dv.inc(vector.tensor_tensor(out=ln_scr[0:1, :], in0=stat_row[0:1, :],
                                            in1=stat_row[0:1, :], op=ALU.mult))
                dv.inc(vector.tensor_tensor(out=stat_row2[0:1, :], in0=stat_row2[0:1, :],
                                            in1=ln_scr[0:1, :], op=ALU.subtract))
            t_v2 = dv.tok()

            @block.scalar
            def _(scalar):
                _w(scalar, t_v2)
                ac.inc(scalar.activation(out=stat_row2[0:1, :], in_=stat_row2[0:1, :], func=AF.Ln,
                                         bias=eps1[:, :]))
                ac.inc(scalar.activation(out=stat_row2[0:1, :], in_=stat_row2[0:1, :], func=AF.Exp,
                                         scale=-0.5))
            t_r2 = ac.tok()

            @block.vector
            def _(vector):
                _w(vector, t_r2)
                dv.inc(vector.tensor_tensor(out=stat_row[0:1, :], in0=stat_row[0:1, :],
                                            in1=stat_row2[0:1, :], op=ALU.mult))
                dv.inc(vector.tensor_scalar(out=stat_row[0:1, :], in0=stat_row[0:1, :],
                                            scalar1=-1.0, scalar2=None, op0=ALU.mult))
            t_am2 = dv.tok()

            bc2_prev = None
            for half in range(2):
                @block.tensor
                def _(tensor, half=half, bc2_prev=bc2_prev):
                    _w(tensor, t_am2, bc2_prev)
                    pe.inc(tensor.matmul(p_io0[:, 0:512], ones_row[:, :],
                                         stat_row2[0:1, half * 512:(half + 1) * 512], start=True, stop=True))
                    pe.inc(tensor.matmul(p_io1[:, 0:512], ones_row[:, :],
                                         stat_row[0:1, half * 512:(half + 1) * 512], start=True, stop=True))
                t_b3 = pe.tok()

                @block.vector
                def _(vector, half=half, t_b3=t_b3):
                    _w(vector, t_b3)
                    dv.inc(vector.tensor_copy(out=rstd1_b[:, half * 512:(half + 1) * 512],
                                              in_=p_io0[:, 0:512]))
                    dv.inc(vector.tensor_copy(out=am1_b[:, half * 512:(half + 1) * 512],
                                              in_=p_io1[:, 0:512]))
                bc2_prev = dv.tok()
            t_ln2 = dv.tok()

            pass
        barrier()

        with ExitStack() as R7:
            E7 = R7.enter_context
            x2ld = [E7(nc.sbuf_tensor(f"x2ld{i}", [128, T], F32)) for i in range(2)]
            out_all = E7(nc.sbuf_tensor("out_all", [128, 16384], F32))
            x2ld_t = [None, None]
            x2n_t = [None, None]
            otp_free = [None, None]
            last_ot = None
            for m in range(16):
                par = m % 2

                @block.sync
                def _(sync, m=m, par=par):
                    _w(sync, x2n_t[par])
                    sq.inc(sync.dma_start(out=x2ld[par][:, :], in_=x2_dram.ap()[:, m * T:(m + 1) * T]), 16)
                x2ld_t[par] = sq.tok()

                @block.vector
                def _(vector, m=m, par=par):
                    _w(vector, x2ld_t[par], t_ln2)
                    dv.inc(vector.tensor_tensor(out=x2ld[par][:, :], in0=x2ld[par][:, :],
                                                in1=rstd1_b[:, :], op=ALU.mult))
                    dv.inc(vector.tensor_tensor(out=x2ld[par][:, :], in0=x2ld[par][:, :],
                                                in1=am1_b[:, :], op=ALU.add))
                t_xn = dv.tok()
                x2n_t[par] = t_xn

                for grp in range(2):
                    bank = [p_y0, p_y1][grp]

                    @block.tensor
                    def _(tensor, m=m, par=par, grp=grp, bank=bank, t_xn=t_xn):
                        _w(tensor, t_xn, otp_free[grp])
                        for j in range(4):
                            i = grp * 4 + j
                            pe.inc(tensor.transpose(bank[0:128, j * 128:(j + 1) * 128],
                                                    x2ld[par][:, i * 128:(i + 1) * 128],
                                                    idnf_s[:, :]))
                    t_otp = pe.tok()

                    @block.vector
                    def _(vector, m=m, grp=grp, bank=bank, t_otp=t_otp):
                        _w(vector, t_otp)
                        for j in range(4):
                            i = grp * 4 + j
                            dv.inc(vector.tensor_copy(out=out_all[:, i * 2048 + m * 128: i * 2048 + m * 128 + 128],
                                                      in_=bank[0:128, j * 128:(j + 1) * 128]))
                    otp_free[grp] = dv.tok()
                x2n_t[par] = dv.tok()
            t_oall = dv.tok()

            @block.sync
            def _(sync):
                _w(sync, t_oall)
                for i in range(8):
                    sq.inc(sync.dma_start(out=out_d[i * 128:(i + 1) * 128, :],
                                          in_=out_all[:, i * 2048:(i + 1) * 2048]), 16)
                sync.wait_ge(sq.sem, sq.n)
                for e in ENGS:
                    if e.n and e is not sq:
                        sync.wait_ge(e.sem, e.n)
    return nc


def _prep_inputs(inputs):
    import ml_dtypes
    u = np.asarray(inputs["L0_attn_u"])[0]
    W_in = np.asarray(inputs["L0_attn_W_in"])
    conv_w = np.asarray(inputs["L0_attn_conv_w"])
    conv_b = np.asarray(inputs["L0_attn_conv_b"])
    dt_bias = np.asarray(inputs["L0_attn_dt_bias"])
    A_log = np.asarray(inputs["L0_attn_A_log"])
    D_param = np.asarray(inputs["L0_attn_D_param"])
    norm_w = np.asarray(inputs["L0_attn_norm_w"])
    W_out = np.asarray(inputs["L0_attn_W_out"])
    w1 = np.asarray(inputs["L0_ffn_w1"])
    w3 = np.asarray(inputs["L0_ffn_w3"])
    w2 = np.asarray(inputs["L0_ffn_w2"])

    WinT = np.ascontiguousarray(W_in.T)
    WoutT = np.ascontiguousarray(W_out.T)
    w1T = np.ascontiguousarray(w1.T)
    w3T = np.ascontiguousarray(w3.T)
    w2T = np.ascontiguousarray(w2.T)

    cw = conv_w[:, 0, :]
    taps = np.concatenate([cw[:, k].reshape(34, 128).T for k in range(4)], axis=1)
    conv_b_cm = np.ascontiguousarray(conv_b.reshape(34, 128).T)
    D_cm = np.ascontiguousarray(np.repeat(D_param, 64).reshape(32, 128).T)
    normw_cm = np.ascontiguousarray(norm_w.reshape(32, 128).T)
    idnf_a = np.eye(128, dtype=np.float32)
    idnb_a = np.eye(128, dtype=np.float32).astype(ml_dtypes.bfloat16)
    s_idx = np.arange(64)[:, None]
    l_idx = np.arange(64)[None, :]
    m64 = np.where(s_idx <= l_idx, 0.0, -1e30).astype(np.float32)
    maskneg = m64.astype(ml_dtypes.bfloat16)

    def wsl_for(r):
        parts = []
        for w in (w1, w2, w3):
            flat = w.reshape(-1)
            n8 = flat.size // NCORE
            sl = flat[r * n8:(r + 1) * n8].reshape(128, -1)
            pad = np.zeros((128, SLICE_W - sl.shape[1]), np.float32)
            parts.append(np.concatenate([sl, pad], axis=1))
        return np.ascontiguousarray(np.concatenate(parts, axis=1))

    shared = dict(
        WinT=WinT, WoutT=WoutT, w1T=w1T, w3T=w3T, w2T=w2T,
        conv_taps=np.ascontiguousarray(taps), conv_b_cm=conv_b_cm, D_cm=D_cm,
        normw_cm=normw_cm, dt_bias_c=np.ascontiguousarray(dt_bias.reshape(64, 1)),
        A_log_c=np.ascontiguousarray(A_log.reshape(64, 1)),
        idnf=idnf_a, idnb=idnb_a, masknegd=np.ascontiguousarray(maskneg),
    )
    in_maps = []
    for r in range(NCORE):
        up = np.zeros((HC, D), np.float32)
        up[3:] = u[r * T:(r + 1) * T]
        if r > 0:
            up[0:3] = u[r * T - 3:r * T]
        wmT = np.zeros((8, 8), np.float32)
        for j in range(8):
            for q in range(8):
                wmT[j, q] = 1.0 if (q < j < r) else 0.0
        qm = (np.arange(8) < r).astype(np.float32).reshape(8, 1)
        m = dict(shared)
        m.update(u_pad=np.ascontiguousarray(up), wslice=wsl_for(r), wmaskT=wmT, qmask=qm)
        in_maps.append(m)
    return in_maps


_NC_CACHE = {}


def _get_nc(tap=None):
    if tap not in _NC_CACHE:
        _NC_CACHE[tap] = build(tap)
    return _NC_CACHE[tap]


def kernel(**inputs):
    in_maps = _prep_inputs(inputs)
    nc = _get_nc(None)
    res = run_bass_kernel_spmd(nc, in_maps, core_ids=list(range(NCORE)))
    outs = [np.asarray(res.results[r]["out_d"], np.float32) for r in range(NCORE)]
    return np.concatenate(outs, axis=0)[None, :, :]


def run_tap(tap, inputs):
    in_maps = _prep_inputs(inputs)
    nc = _get_nc(tap)
    res = run_bass_kernel_spmd(nc, in_maps, core_ids=list(range(NCORE)))
    return res
